# version 02
# baseline (speedup 1.0000x reference)
import os
import numpy as np
import ml_dtypes
import concourse.bass as bass
import concourse.mybir as mybir
from concourse.bass_utils import run_bass_kernel_spmd

F32 = mybir.dt.float32
BF16 = mybir.dt.bfloat16
AF = mybir.ActivationFunctionType
ALU = mybir.AluOpType
AX = mybir.AxisListType
bf16 = ml_dtypes.bfloat16

D = 2048
H = 16
KH = 4
HD = 128
WIN = 512
T = 2048
TH = 2560
NT = 20
NQ = 16
FF = 8192
NFC = 64
FCH = 16
SC = 0.08838834764831845
RMS_EPS = 1e-6
LN_EPS = 1e-5
Q_EPS = 1e-8
KB = 1024
DBG = bool(int(os.environ.get("ATH_DBG", "0")))


def _host_constants(half):
    start = half * T - WIN
    pos = (np.arange(TH, dtype=np.float64) + start).astype(np.float32)
    inv_freq = (1.0 / 10000.0 ** (np.arange(0, HD, 2, dtype=np.float32) / HD)).astype(np.float32)
    ang = (pos[:, None] * inv_freq[None, :]).astype(np.float32)
    cos = np.cos(ang.astype(np.float64)).astype(np.float32)
    sin = np.sin(ang.astype(np.float64)).astype(np.float32)
    c2 = np.zeros((128, NT * 128), np.float32)
    s2 = np.zeros((128, NT * 128), np.float32)
    for ti in range(NT):
        cs = cos[ti * 128:(ti + 1) * 128]
        sn = sin[ti * 128:(ti + 1) * 128]
        c2[:, ti * 128:(ti + 1) * 128:2] = cs
    c2[:] = 0; s2[:] = 0
    for ti in range(NT):
        cs = cos[ti * 128:(ti + 1) * 128]
        sn = sin[ti * 128:(ti + 1) * 128]
        blk_c = np.empty((128, 128), np.float32)
        blk_s = np.empty((128, 128), np.float32)
        blk_c[:, 0::2] = cs
        blk_c[:, 1::2] = cs
        blk_s[:, 0::2] = -sn
        blk_s[:, 1::2] = sn
        c2[:, ti * 128:(ti + 1) * 128] = blk_c
        s2[:, ti * 128:(ti + 1) * 128] = blk_s
    p = np.arange(128)[:, None]
    c = np.arange(640)[None, :]
    base = np.where((c >= p) & (c <= p + WIN), 0.0, -1e30).astype(np.float32)
    masks = np.zeros((128, 5 * 640), np.float32)
    for cls in range(5):
        m = base.copy()
        if half == 0 and cls < 4:
            m = m + np.where(c >= WIN - cls * 128, 0.0, -1e30)
        masks[:, cls * 640:(cls + 1) * 640] = np.maximum(m, -1e30)
    return c2.astype(bf16), s2.astype(bf16), masks.astype(bf16)


def _stage_inputs(L0_attn_x, L0_attn_ln_w, L0_attn_wq, L0_attn_wk, L0_attn_wv,
                  L0_attn_wo, L0_ffn_w1, L0_ffn_b1, L0_ffn_w2, L0_ffn_b2):
    xf = np.ascontiguousarray(np.asarray(L0_attn_x, np.float32).reshape(4 * 4096, D))
    w1 = np.asarray(L0_ffn_w1, np.float32)
    w1s = np.ascontiguousarray(
        w1.reshape(NFC, 128, FCH, 128).transpose(0, 3, 2, 1).reshape(FF, D))
    w2t = np.ascontiguousarray(np.asarray(L0_ffn_w2, np.float32).T)
    lnw_rep = np.ascontiguousarray(
        np.broadcast_to(np.asarray(L0_attn_ln_w, np.float32)[None, :], (128, D)))
    b2_rep = np.ascontiguousarray(
        np.broadcast_to(np.asarray(L0_ffn_b2, np.float32)[None, :], (128, D)))
    b1s = np.ascontiguousarray(np.asarray(L0_ffn_b1, np.float32).reshape(NFC, 128).T)
    ident = np.eye(128, dtype=bf16)
    onesr = np.ones((1, 128), np.float32)
    csts = np.zeros((128, 8), np.float32)
    csts[:, 0] = RMS_EPS
    csts[:, 1] = LN_EPS
    csts[:, 2] = Q_EPS
    csts[:, 3] = 1.0

    shared = dict(
        wq=np.ascontiguousarray(np.asarray(L0_attn_wq, np.float32)),
        wk=np.ascontiguousarray(np.asarray(L0_attn_wk, np.float32)),
        wv=np.ascontiguousarray(np.asarray(L0_attn_wv, np.float32)),
        wo=np.ascontiguousarray(np.asarray(L0_attn_wo, np.float32)),
        w1s=w1s, w2t=w2t, lnw_rep=lnw_rep, b2_rep=b2_rep, b1s=b1s,
        ident=ident, onesr=onesr, csts=csts,
    )
    in_maps = []
    for core in range(8):
        b, half = core >> 1, core & 1
        rows0 = b * 4096 + half * T
        x_sh = np.zeros((TH, D), np.float32)
        if half == 0:
            x_sh[WIN:] = xf[rows0:rows0 + T]
        else:
            x_sh[:] = xf[rows0 - WIN:rows0 + T]
        c2, s2, masks = _host_constants(half)
        m = dict(shared)
        m.update(x_sh=np.ascontiguousarray(x_sh), c2t=np.ascontiguousarray(c2),
                 s2t=np.ascontiguousarray(s2), masks=np.ascontiguousarray(masks))
        in_maps.append(m)
    return in_maps


class P:
    ENGS = ("sync", "gpsimd", "tensor", "vector", "scalar")

    def __init__(self, nc, sems, engs, cur):
        self.nc = nc
        self.cur = cur
        for e in P.ENGS:
            setattr(self, e, engs[e] if e == cur else None)
        self.sems = sems
        self.cnt = {k: 0 for k in sems}
        self._lastw = {}

    def inc(self, sem, instr, k=1):
        self.cnt[sem] += k
        if instr is not None:
            instr.then_inc(self.sems[sem], k)
        return self.cnt[sem]

    def wait(self, eng, sem, val=None):
        v = self.cnt[sem] if val is None else val
        if v is None or v <= 0:
            return
        if getattr(self, eng) is not None and self._lastw.get((eng, sem), -1) < v:
            getattr(self, eng).wait_ge(self.sems[sem], v)
            self._lastw[(eng, sem)] = v

    def snap(self):
        return dict(self.cnt)


def build_nc():
    nc = bass.Bass("TRN2", target_bir_lowering=False, detect_race_conditions=False)
    dt = {}

    def din(name, shape, dtype=F32):
        dt[name] = nc.dram_tensor(name, list(shape), dtype, kind="ExternalInput").ap()

    def dout(name, shape, dtype=F32):
        dt[name] = nc.dram_tensor(name, list(shape), dtype, kind="ExternalOutput").ap()

    def dint(name, shape, dtype):
        dt[name] = nc.dram_tensor(name, list(shape), dtype, kind="Internal").ap()

    din("x_sh", (TH, D))
    din("wq", (D, D)); din("wk", (D, 512)); din("wv", (D, 512)); din("wo", (D, D))
    din("w1s", (FF, D)); din("w2t", (FF, D))
    din("lnw_rep", (128, D)); din("b2_rep", (128, D)); din("b1s", (128, NFC))
    din("ident", (128, 128), BF16); din("onesr", (1, 128)); din("csts", (128, 8))
    din("c2t", (128, NT * 128), BF16); din("s2t", (128, NT * 128), BF16)
    din("masks", (128, 5 * 640), BF16)
    dout("out_sh", (T, D))
    if DBG:
        dout("dbg_y", (T, D))
        dout("dbg_av", (T, D), BF16)
        dout("dbg_kt", (128, NT * 512), BF16)
        dout("dbg_v", (128, NT * 512), BF16)
        dout("dbg_qt", (NQ * 128, D), BF16)
        dout("dbg_pn", (128, 640), BF16)
        dout("dbg_pt", (128, 640), BF16)
        dout("dbg_sc", (128, 640))
        dout("dbg_o2", (T, D))
    dint("y_d", (T, D), F32)
    dint("yt_d", (D, T), BF16)
    dint("av_d", (T, D), BF16)
    dint("w1q_d", (FF, D), BF16)
    dint("w2q_d", (FF, D), BF16)

    sb = {}
    LIMIT = 224 * KB - 64

    def S(name, shape, dtype, off):
        n = int(np.prod(shape[1:])) * (2 if dtype == BF16 else 4)
        assert off + n <= LIMIT, (name, off, n)
        sb[name] = nc.alloc_sbuf_tensor_at(name, list(shape), dtype, offset=off)
        return off + ((n + 31) & ~31)

    o = 17 * KB
    o = S("ident_sb", (128, 128), BF16, o)
    o = S("b1_sb", (128, NFC), F32, o)
    o = S("cst_sb", (128, 8), F32, o)
    o = S("onesr_sb", (1, 128), F32, o)
    o = S("rstd_sb", (128, NT), F32, o)
    o = S("th_sb", (128, 4), F32, o)
    o = S("sum1_sb", (128, NFC), F32, o)
    o = S("red_sb", (128, 2), F32, o)
    o = S("tot_sb", (1, 2), F32, o)
    o = S("sval_sb", (128, 2), F32, o)
    o = S("b2r_sb", (128, D), F32, o)
    o = S("zp_sb", (128, 8), F32, o)
    o = S("nm2_sb", (128, 4), F32, o)
    o = S("nm1_sb", (128, 2), F32, o)
    o = S("nms_sb", (128, 2), F32, o)
    o = S("rows_sb", (128, 2), F32, o)
    o = S("rec_sb", (128, 2), F32, o)
    o = S("st1_sb", (128, 4), F32, o)
    SBASE = o

    o = S("kT_sb", (128, NT * 512), BF16, o)
    o = S("v_sb", (128, NT * 512), BF16, o)
    o = S("c2_sb", (128, NT * 128), BF16, o)
    o = S("s2_sb", (128, NT * 128), BF16, o)
    o = S("masks_sb", (128, 5 * 640), BF16, o)
    o = S("lnw_sb", (128, D), F32, o)
    o_x0 = o
    o = S("x0_sb", (128, D), F32, o)
    o = S("x1_sb", (128, D), F32, o)
    o_xn = o
    o = S("xn_sb", (128, D), BF16, o)
    o = S("xnT_sb", (128, D), BF16, o)
    o_sq = o
    o = S("sq_sb", (128, D), F32, o)
    o_t1 = o
    o = S("t1_sb", (128, 1024), F32, o)
    o = S("t2_sb", (128, 1024), F32, o)
    o_kr = o
    o = S("krot_sb", (128, 512), BF16, o)
    WSLOT = o
    o = S("wq_sb", (128, FCH * D), BF16, o)
    o_q = o
    o = S("q_sb", (128, D), BF16, o)
    o = S("qT_sb", (128, D), BF16, o)
    o_pr = o
    o = S("pr0_sb", (128, 640), BF16, o)
    o = S("pr1_sb", (128, 640), BF16, o)
    o_pn = o
    o = S("pn0_sb", (128, 640), BF16, o)
    o = S("pn1_sb", (128, 640), BF16, o)
    o = S("pt0_sb", (128, 640), BF16, o)
    o = S("pt1_sb", (128, 640), BF16, o)
    o_av = o
    o = S("av0_sb", (128, D), BF16, o)
    o = S("av1_sb", (128, D), BF16, o)
    ATT_END = o

    o2 = WSLOT
    o2 = S("ws0_sb", (128, D), F32, o2)
    o2 = S("ws1_sb", (128, D), F32, o2)
    o2 = S("sga_sb", (128, D), BF16, o2)
    o2 = S("sgb_sb", (128, D), BF16, o2)
    o2 = S("q2_sb", (128, D), BF16, o2)
    o2 = WSLOT
    o2 = S("wk_sb", (128, FCH * 512), BF16, o2)
    o2 = S("wv_sb", (128, FCH * 512), BF16, o2)
    S("wo_sb", (128, FCH * D), BF16, WSLOT)
    S("avb0_sb", (128, D), BF16, o_av)
    S("avb1_sb", (128, D), BF16, o_av + 4 * KB)
    S("z_sb", (128, D), F32, o_t1)
    S("y_sb", (128, D), F32, o_xn)
    S("ybf_sb", (128, D), BF16, o_q)
    S("ytt_sb", (128, D), BF16, o_pn)

    o = SBASE
    o = S("out2_sb", (128, 4 * D), F32, o)
    o = S("yT_sb", (128, FCH * 512), BF16, o)
    o = S("h_sb", (128, NFC * 512), BF16, o)
    for i in range(3):
        o = S(f"w1r{i}_sb", (128, D), BF16, o)
    for i in range(3):
        o = S(f"w2r{i}_sb", (128, 4 * 512), BF16, o)
    o = S("fsq_sb", (128, D), F32, o)
    o = S("of0_sb", (128, D), F32, o)
    o = S("of1_sb", (128, D), F32, o)
    assert o <= LIMIT and ATT_END <= LIMIT, (o, ATT_END)

    ps = [nc.alloc_psum_tensor(f"P{i}", [128, 1024], F32) for i in range(4)]
    sems = {n: nc.alloc_semaphore(name=n) for n in
            ["ld", "cast", "st", "pe", "dv", "ac", "bar"]}
    engs_map = dict(sync=nc.sync, gpsimd=nc.gpsimd, tensor=nc.tensor,
                    vector=nc.vector, scalar=nc.scalar)

    with nc.Block() as block:
        @block.sync
        def _(eng):
            emit_all(P(nc, sems, engs_map, "sync"), dt, sb, ps)

        @block.gpsimd
        def _(eng):
            emit_all(P(nc, sems, engs_map, "gpsimd"), dt, sb, ps)

        @block.tensor
        def _(eng):
            emit_all(P(nc, sems, engs_map, "tensor"), dt, sb, ps)

        @block.vector
        def _(eng):
            emit_all(P(nc, sems, engs_map, "vector"), dt, sb, ps)

        @block.scalar
        def _(eng):
            emit_all(P(nc, sems, engs_map, "scalar"), dt, sb, ps)

    return nc


def emit_all(p, dt, sb, ps):
    psb = [t[:, :].bitcast(BF16) for t in ps]
    psf = [t[:, :] for t in ps]

    for (dst, src) in [("ident_sb", "ident"), ("b1_sb", "b1s"), ("cst_sb", "csts"),
                       ("onesr_sb", "onesr"), ("b2r_sb", "b2_rep"),
                       ("c2_sb", "c2t"), ("s2_sb", "s2t"), ("masks_sb", "masks"),
                       ("lnw_sb", "lnw_rep")]:
        p.inc("ld", p.sync and p.sync.dma_start(out=sb[dst][:, :], in_=dt[src][:, :]), 16)
    lds_consts = p.cnt["ld"]

    ring = [sb["ws0_sb"], sb["ws1_sb"]]
    red_done = {}
    for wi, src in enumerate(["w1s", "w2t"]):
        for s in range(NFC):
            slot = s % 2
            key = (wi, s)
            prevk = (wi, s - 2) if s >= 2 else ((wi - 1, NFC - 2 + s) if wi > 0 else None)
            if prevk in red_done:
                p.wait("sync", "dv", red_done[prevk])
            v = p.inc("ld", p.sync and p.sync.dma_start(
                out=ring[slot][:, :], in_=dt[src][s * 128:(s + 1) * 128, :]), 16)
            p.wait("vector", "ld", v)
            red_done[key] = p.inc("dv", p.vector and p.vector.reduce_sum(
                sb["sum1_sb"][:, s:s + 1], ring[slot][:, :], axis=AX.X,
                apply_absolute_value=True))
        p.wait("vector", "dv", red_done[(wi, NFC - 1)])
        v = p.inc("dv", p.vector and p.vector.tensor_reduce(
            sb["red_sb"][:, wi:wi + 1], sb["sum1_sb"][:, 0:NFC], AX.X, ALU.add))
        p.wait("vector", "dv", v)
    p.wait("tensor", "dv")
    p.wait("tensor", "ld", lds_consts)
    vpe = p.inc("pe", p.tensor and p.tensor.matmul(
        psf[0][0:1, 4:6], sb["cst_sb"][:, 3:4], sb["red_sb"][:, 0:2],
        start=True, stop=True))
    p.wait("scalar", "pe", vpe)
    vac = p.inc("ac", p.scalar and p.scalar.copy(sb["tot_sb"][0:1, 0:2], psf[0][0:1, 4:6]))
    p.wait("tensor", "ac", vac)
    vpe = p.inc("pe", p.tensor and p.tensor.matmul(
        psf[0][:, 8:10], sb["onesr_sb"][0:1, :], sb["tot_sb"][0:1, 0:2],
        start=True, stop=True))
    p.wait("vector", "pe", vpe)
    v = p.inc("dv", p.vector and p.vector.tensor_scalar(
        sb["sval_sb"][:, 0:2], psf[0][:, 8:10], 1.0 / (FF * D), sb["cst_sb"][:, 2:3],
        ALU.mult, ALU.max))
    p.wait("vector", "dv", v)
    v = p.inc("dv", p.vector and p.vector.tensor_scalar(
        sb["th_sb"][:, 0:2], sb["sval_sb"][:, 0:2], 0.5, None, ALU.mult))
    p.wait("vector", "dv", v)
    v = p.inc("dv", p.vector and p.vector.tensor_scalar(
        sb["th_sb"][:, 2:4], sb["sval_sb"][:, 0:2], -0.5, None, ALU.mult))
    th_ready = p.cnt["dv"]

    qst = {}
    for wi, (src, dst) in enumerate([("w1s", "w1q_d"), ("w2t", "w2q_d")]):
        for s in range(NFC):
            slot = s % 2
            key = (wi, s)
            prevk = (wi, s - 2) if s >= 2 else ((wi - 1, NFC - 2 + s) if wi > 0 else None)
            if prevk in qst:
                p.wait("sync", "ac", qst[prevk]["sg"])
            if wi == 0 and s < 2:
                p.wait("sync", "dv", red_done[(1, NFC - 2 + s)])
            vld = p.inc("ld", p.sync and p.sync.dma_start(
                out=ring[slot][:, :], in_=dt[src][s * 128:(s + 1) * 128, :]), 16)
            p.wait("scalar", "ld", vld)
            p.wait("scalar", "dv", th_ready)
            prev1 = (wi, s - 1) if s >= 1 else ((wi - 1, NFC - 1) if wi > 0 else None)
            if prev1 in qst:
                p.wait("scalar", "dv", qst[prev1]["add"])
            p.scalar and p.scalar.activation(
                sb["sga_sb"][:, :], ring[slot][:, :], AF.Sign,
                bias=sb["th_sb"][:, 2 + wi:3 + wi], scale=1.0)
            vsg = p.inc("ac", p.scalar and p.scalar.activation(
                sb["sgb_sb"][:, :], ring[slot][:, :], AF.Sign,
                bias=sb["th_sb"][:, wi:wi + 1], scale=1.0))
            p.wait("vector", "ac", vsg)
            if prev1 in qst:
                p.wait("vector", "st", qst[prev1]["st"])
            vadd = p.inc("dv", p.vector and p.vector.tensor_add(
                sb["q2_sb"][:, :], sb["sga_sb"][:, :], sb["sgb_sb"][:, :]))
            p.wait("sync", "dv", vadd)
            vst = p.inc("st", p.sync and p.sync.dma_start(
                out=dt[dst][s * 128:(s + 1) * 128, :], in_=sb["q2_sb"][:, :]), 16)
            qst[key] = dict(sg=vsg, add=vadd, st=vst)
    wq_done_st = p.cnt["st"]
    wq_done_dv = p.cnt["dv"]

    p.wait("gpsimd", "st", wq_done_st)
    p.wait("gpsimd", "dv", wq_done_dv)
    wk3 = dt["wk"].rearrange("(c p) n -> p c n", p=128)
    p.inc("cast", p.gpsimd and p.gpsimd.dma_start(
        out=sb["wk_sb"][:, :].rearrange("p (c n) -> p c n", c=FCH), in_=wk3), 16)
    wv3 = dt["wv"].rearrange("(c p) n -> p c n", p=128)
    p.inc("cast", p.gpsimd and p.gpsimd.dma_start(
        out=sb["wv_sb"][:, :].rearrange("p (c n) -> p c n", c=FCH), in_=wv3), 16)
    kv_w_ready = p.cnt["cast"]

    x3 = dt["x_sh"].rearrange("(t p) d -> t p d", p=128)
    xring = [sb["x0_sb"], sb["x1_sb"]]

    ast = {}
    for ti in range(NT):
        slot = ti % 2
        if ti >= 2:
            p.wait("sync", "dv", ast[ti - 2]["xn"])
        vx = p.inc("ld", p.sync and p.sync.dma_start(
            out=xring[slot][:, :], in_=x3[ti]), 16)
        p.wait("scalar", "ld", vx)
        p.wait("scalar", "ld", lds_consts)
        p.scalar and p.scalar.activation(
            sb["sq_sb"][:, :], xring[slot][:, :], AF.Square, accum_out=sb["st1_sb"][:, 0:1])
        p.scalar and p.scalar.activation(
            sb["st1_sb"][:, 1:2], sb["st1_sb"][:, 0:1], AF.Ln,
            bias=sb["cst_sb"][:, 0:1], scale=1.0 / D)
        vr = p.inc("ac", p.scalar and p.scalar.activation(
            sb["rstd_sb"][:, ti:ti + 1], sb["st1_sb"][:, 1:2], AF.Exp, scale=-0.5))
        p.wait("vector", "ac", vr)
        p.wait("vector", "ld", lds_consts)
        if ti >= 1:
            p.wait("vector", "pe", ast[ti - 1]["trx"])
        vxn = p.inc("dv", p.vector and p.vector.scalar_tensor_tensor(
            sb["xn_sb"][:, :], xring[slot][:, :], sb["rstd_sb"][:, ti:ti + 1],
            sb["lnw_sb"][:, :], ALU.mult, ALU.mult))
        p.wait("tensor", "dv", vxn)
        if ti >= 1:
            p.wait("tensor", "dv", ast[ti - 1]["xnT_ev"])
        ins = None
        for fch in range(FCH):
            ins = p.tensor and p.tensor.transpose(
                psb[2][:, fch * 128:(fch + 1) * 128],
                sb["xn_sb"][:, fch * 128:(fch + 1) * 128], sb["ident_sb"][:, :])
        vtrx = p.inc("pe", ins)
        p.wait("vector", "pe", vtrx)
        if ti >= 1:
            p.wait("vector", "pe", ast[ti - 1]["kv"])
        vxnT = p.inc("dv", p.vector and p.vector.tensor_copy(
            sb["xnT_sb"][:, :], psb[2][:, 0:D]))
        p.wait("tensor", "dv", vxnT)
        p.wait("tensor", "cast", kv_w_ready)
        if ti >= 1:
            p.wait("tensor", "dv", ast[ti - 1]["rope_rd"])
            p.wait("tensor", "ac", ast[ti - 1]["vev"])
        ins = None
        for fch in range(FCH):
            p.tensor and p.tensor.matmul(
                psf[0][:, 0:512], sb["xnT_sb"][:, fch * 128:(fch + 1) * 128],
                sb["wk_sb"][:, fch * 512:(fch + 1) * 512],
                start=(fch == 0), stop=(fch == FCH - 1))
            ins = p.tensor and p.tensor.matmul(
                psf[0][:, 512:1024], sb["xnT_sb"][:, fch * 128:(fch + 1) * 128],
                sb["wv_sb"][:, fch * 512:(fch + 1) * 512],
                start=(fch == 0), stop=(fch == FCH - 1))
        vkv = p.inc("pe", ins)
        p.wait("vector", "pe", vkv)
        kview = psf[0][:, 0:512].rearrange("p (h c) -> p h c", h=KH)
        c2s = sb["c2_sb"][:, ti * 128:(ti + 1) * 128].unsqueeze(1).broadcast_to([128, KH, 128])
        p.vector and p.vector.tensor_tensor(
            sb["t1_sb"][:, 0:512].rearrange("p (h c) -> p h c", h=KH), kview, c2s, ALU.mult)
        kv4 = psf[0][:, 0:512].rearrange("p (h i pr) -> p h i pr", h=KH, pr=2)
        t24 = sb["t2_sb"][:, 0:512].rearrange("p (h i pr) -> p h i pr", h=KH, pr=2)
        s24 = sb["s2_sb"][:, ti * 128:(ti + 1) * 128].rearrange(
            "p (i pr) -> p i pr", pr=2).unsqueeze(1).broadcast_to([128, KH, 64, 2])
        p.vector and p.vector.tensor_tensor(
            t24[:, :, :, 0:1], kv4[:, :, :, 1:2], s24[:, :, :, 0:1], ALU.mult)
        vrope_rd = p.inc("dv", p.vector and p.vector.tensor_tensor(
            t24[:, :, :, 1:2], kv4[:, :, :, 0:1], s24[:, :, :, 1:2], ALU.mult))
        p.wait("vector", "dv", vrope_rd)
        if ti >= 1:
            p.wait("vector", "pe", ast[ti - 1]["trk"])
        vkr = p.inc("dv", p.vector and p.vector.tensor_add(
            sb["krot_sb"][:, :], sb["t1_sb"][:, 0:512], sb["t2_sb"][:, 0:512]))
        p.wait("tensor", "dv", vkr)
        if ti >= 1:
            p.wait("tensor", "dv", ast[ti - 1]["ktr_ev"])
        ins = None
        for kh in range(KH):
            ins = p.tensor and p.tensor.transpose(
                psb[1][:, kh * 128:(kh + 1) * 128],
                sb["krot_sb"][:, kh * 128:(kh + 1) * 128], sb["ident_sb"][:, :])
        vtrk = p.inc("pe", ins)
        p.wait("vector", "pe", vtrk)
        vktr = p.inc("dv", p.vector and p.vector.tensor_copy(
            sb["kT_sb"][:, ti * 512:(ti + 1) * 512], psb[1][:, 0:512]))
        p.wait("scalar", "pe", vkv)
        vvev = p.inc("ac", p.scalar and p.scalar.copy(
            sb["v_sb"][:, ti * 512:(ti + 1) * 512], psf[0][:, 512:1024]))
        ast[ti] = dict(xn=vxn, trx=vtrx, xnT_ev=vxnT, rope_rd=vrope_rd,
                       vev=vvev, trk=vtrk, ktr_ev=vktr, kv=vkv)
    a_end = p.snap()

    p.wait("gpsimd", "pe", a_end["pe"])
    wq3 = dt["wq"].rearrange("(c p) n -> p c n", p=128)
    p.inc("cast", p.gpsimd and p.gpsimd.dma_start(
        out=sb["wq_sb"][:, :].rearrange("p (c n) -> p c n", c=FCH), in_=wq3), 16)
    wq_ready = p.cnt["cast"]

    if DBG:
        p.wait("sync", "dv", a_end["dv"])
        p.wait("sync", "ac", a_end["ac"])
        p.inc("st", p.sync and p.sync.dma_start(out=dt["dbg_kt"][:, :], in_=sb["kT_sb"][:, :]), 16)
        p.inc("st", p.sync and p.sync.dma_start(out=dt["dbg_v"][:, :], in_=sb["v_sb"][:, :]), 16)
    prs = [sb["pr0_sb"], sb["pr1_sb"]]
    pns = [sb["pn0_sb"], sb["pn1_sb"]]
    pts = [sb["pt0_sb"], sb["pt1_sb"]]
    avs = [sb["av0_sb"], sb["av1_sb"]]
    b1 = {}
    hs = {}
    for qt in range(NQ):
        ti = qt + 4
        slot = ti % 2
        if qt >= 2:
            p.wait("sync", "dv", b1[qt - 2]["xn"])
        else:
            p.wait("sync", "dv", a_end["dv"])
        vx = p.inc("ld", p.sync and p.sync.dma_start(
            out=xring[slot][:, :], in_=x3[ti]), 16)
        p.wait("vector", "ld", vx)
        if qt >= 1:
            p.wait("vector", "pe", b1[qt - 1]["trx"])
        vxn = p.inc("dv", p.vector and p.vector.scalar_tensor_tensor(
            sb["xn_sb"][:, :], xring[slot][:, :], sb["rstd_sb"][:, ti:ti + 1],
            sb["lnw_sb"][:, :], ALU.mult, ALU.mult))
        p.wait("tensor", "dv", vxn)
        if qt >= 1:
            p.wait("tensor", "dv", b1[qt - 1]["xnT_ev"])
        ins = None
        for fch in range(FCH):
            ins = p.tensor and p.tensor.transpose(
                psb[2][:, fch * 128:(fch + 1) * 128],
                sb["xn_sb"][:, fch * 128:(fch + 1) * 128], sb["ident_sb"][:, :])
        vtrx = p.inc("pe", ins)
        p.wait("vector", "pe", vtrx)
        if qt >= 1:
            p.wait("vector", "pe", b1[qt - 1]["qp"])
        vxnT = p.inc("dv", p.vector and p.vector.tensor_copy(
            sb["xnT_sb"][:, :], psb[2][:, 0:D]))
        p.wait("tensor", "dv", vxnT)
        p.wait("tensor", "cast", wq_ready)
        if qt >= 1:
            p.wait("tensor", "dv", b1[qt - 1]["rope_rd"])
        ins = None
        for fch in range(FCH):
            for ng in range(4):
                tgt = psf[2] if ng < 2 else psf[3]
                ins = p.tensor and p.tensor.matmul(
                    tgt[:, (ng % 2) * 512:(ng % 2) * 512 + 512],
                    sb["xnT_sb"][:, fch * 128:(fch + 1) * 128],
                    sb["wq_sb"][:, fch * D + ng * 512: fch * D + ng * 512 + 512],
                    start=(fch == 0), stop=(fch == FCH - 1))
        vqp = p.inc("pe", ins)
        p.wait("vector", "pe", vqp)
        if qt >= 1:
            p.wait("vector", "pe", b1[qt - 1]["q_done"])
        vrope_rd = None
        for hf in range(2):
            src = psf[2] if hf == 0 else psf[3]
            qv = src[:, 0:1024].rearrange("p (h c) -> p h c", h=8)
            c2s = sb["c2_sb"][:, ti * 128:(ti + 1) * 128].unsqueeze(1).broadcast_to([128, 8, 128])
            p.vector and p.vector.tensor_tensor(
                sb["t1_sb"][:, 0:1024].rearrange("p (h c) -> p h c", h=8), qv, c2s, ALU.mult)
            q4 = src[:, 0:1024].rearrange("p (h i pr) -> p h i pr", h=8, pr=2)
            t24 = sb["t2_sb"][:, 0:1024].rearrange("p (h i pr) -> p h i pr", h=8, pr=2)
            s24 = sb["s2_sb"][:, ti * 128:(ti + 1) * 128].rearrange(
                "p (i pr) -> p i pr", pr=2).unsqueeze(1).broadcast_to([128, 8, 64, 2])
            p.vector and p.vector.tensor_tensor(
                t24[:, :, :, 0:1], q4[:, :, :, 1:2], s24[:, :, :, 0:1], ALU.mult)
            vrope_rd = p.inc("dv", p.vector and p.vector.tensor_tensor(
                t24[:, :, :, 1:2], q4[:, :, :, 0:1], s24[:, :, :, 1:2], ALU.mult))
            p.wait("vector", "dv", vrope_rd)
            p.inc("dv", p.vector and p.vector.tensor_add(
                sb["q_sb"][:, hf * 1024:(hf + 1) * 1024], sb["t1_sb"][:, 0:1024],
                sb["t2_sb"][:, 0:1024]))
        vq_ready = p.cnt["dv"]
        p.wait("tensor", "dv", vq_ready)
        ins = None
        for h in range(H):
            ins = p.tensor and p.tensor.transpose(
                psb[2][:, h * 128:(h + 1) * 128],
                sb["q_sb"][:, h * 128:(h + 1) * 128], sb["ident_sb"][:, :])
        vtrq = p.inc("pe", ins)
        p.wait("vector", "pe", vtrq)
        if qt >= 1:
            p.wait("vector", "pe", b1[qt - 1]["q_done"])
        vqT = p.inc("dv", p.vector and p.vector.tensor_copy(
            sb["qT_sb"][:, :], psb[2][:, 0:D]))
        if DBG:
            p.wait("sync", "dv", vqT)
            p.inc("st", p.sync and p.sync.dma_start(
                out=dt["dbg_qt"][qt * 128:(qt + 1) * 128, :], in_=sb["qT_sb"][:, :]), 16)
        cls = qt if qt < 4 else 4
        for h in range(H):
            pp = h % 2
            pair_f = psf[pp]
            pair_b = psb[pp]
            kv = h // 4
            prev2 = hs.get((qt, h - 2)) or (hs.get((qt - 1, H - 2 + h)) if qt > 0 else None)
            prev1 = hs.get((qt, h - 1)) or (hs.get((qt - 1, H - 1)) if qt > 0 else None)
            p.wait("tensor", "dv", vqT)
            if prev2 is not None:
                p.wait("tensor", "ac", prev2["avev"])
                p.wait("tensor", "ac", prev2["ptev"])
                p.wait("tensor", "dv", prev2["nm"])
            kmov1 = sb["kT_sb"][:, :].rearrange("p (t c) -> p t c", c=512)[
                :, ti - 4:ti, kv * 128:(kv + 1) * 128]
            p.tensor and p.tensor.matmul(
                pair_f[:, 0:512], sb["qT_sb"][:, h * 128:(h + 1) * 128], kmov1,
                start=True, stop=True)
            vsc = p.inc("pe", p.tensor and p.tensor.matmul(
                pair_f[:, 512:640], sb["qT_sb"][:, h * 128:(h + 1) * 128],
                sb["kT_sb"][:, ti * 512 + kv * 128: ti * 512 + kv * 128 + 128],
                start=True, stop=True))
            p.wait("vector", "pe", vsc)
            so = pair_f[:, 0:640].rearrange("p (bk c) -> p bk c", bk=5)
            sm = sb["masks_sb"][:, cls * 640:(cls + 1) * 640].rearrange("p (bk c) -> p bk c", bk=5)
            if qt < 4:
                vmask = p.inc("dv", p.vector and p.vector.tensor_tensor(
                    pair_f[:, 0:640], pair_f[:, 0:640],
                    sb["masks_sb"][:, cls * 640:(cls + 1) * 640], ALU.add))
            else:
                vmask = p.inc("dv", p.vector and p.vector.tensor_tensor(
                    so[:, 0:5:4, :], so[:, 0:5:4, :], sm[:, 0:5:4, :], ALU.add))
            p.wait("vector", "dv", vmask)
            if DBG and qt == 4 and h == 0:
                vdbg = p.inc("dv", p.vector and p.vector.tensor_copy(sb["sq_sb"][:, 0:640], pair_f[:, 0:640]))
                p.wait("sync", "dv", vdbg)
                p.inc("st", p.sync and p.sync.dma_start(out=dt["dbg_sc"][:, :], in_=sb["sq_sb"][:, 0:640]), 16)
            if prev2 is not None:
                p.wait("vector", "ac", prev2["exp"])
            p.vector and p.vector.reduce_max(
                sb["nm2_sb"][:, 2 * pp:2 * pp + 1], pair_f[:, 0:512], axis=AX.X, negate=True)
            vr2 = p.inc("dv", p.vector and p.vector.reduce_max(
                sb["nm2_sb"][:, 2 * pp + 1:2 * pp + 2], pair_f[:, 512:640], axis=AX.X, negate=True))
            p.wait("vector", "dv", vr2)
            vnmm = p.inc("dv", p.vector and p.vector.tensor_reduce(
                sb["nm1_sb"][:, pp:pp + 1], sb["nm2_sb"][:, 2 * pp:2 * pp + 2], AX.X, ALU.min))
            p.wait("vector", "dv", vnmm)
            vnm = p.inc("dv", p.vector and p.vector.tensor_scalar(
                sb["nms_sb"][:, pp:pp + 1], sb["nm1_sb"][:, pp:pp + 1], SC, None, ALU.mult))
            p.wait("scalar", "dv", vnm)
            if prev2 is not None:
                p.wait("scalar", "dv", prev2["pn"])
            vexp = p.inc("ac", p.scalar and p.scalar.activation(
                prs[pp][:, :], pair_f[:, 0:640], AF.Exp, bias=sb["nms_sb"][:, pp:pp + 1],
                scale=SC, accum_out=sb["rows_sb"][:, pp:pp + 1]))
            p.wait("vector", "ac", vexp)
            vrc = p.inc("dv", p.vector and p.vector.reciprocal(
                sb["rec_sb"][:, pp:pp + 1], sb["rows_sb"][:, pp:pp + 1]))
            p.wait("vector", "dv", vrc)
            if prev2 is not None:
                p.wait("vector", "pe", prev2["tr"])
            vpn = p.inc("dv", p.vector and p.vector.tensor_scalar(
                pns[pp][:, :], prs[pp][:, :], sb["rec_sb"][:, pp:pp + 1], None, ALU.mult))
            if DBG and qt == 4 and h == 0:
                p.wait("sync", "dv", vpn)
                p.inc("st", p.sync and p.sync.dma_start(out=dt["dbg_pn"][:, :], in_=pns[pp][:, :]), 16)
            p.wait("tensor", "dv", vpn)
            p.wait("tensor", "ac", vexp)
            ins = None
            for j in range(5):
                ins = p.tensor and p.tensor.transpose(
                    pair_b[:, j * 128:(j + 1) * 128], pns[pp][:, j * 128:(j + 1) * 128],
                    sb["ident_sb"][:, :])
            vtr = p.inc("pe", ins)
            p.wait("scalar", "pe", vtr)
            if prev2 is not None:
                p.wait("scalar", "pe", prev2["av"])
            vptev = p.inc("ac", p.scalar and p.scalar.copy(
                pts[pp][:, :], pair_b[:, 0:640]))
            if DBG and qt == 4 and h == 0:
                p.wait("sync", "ac", vptev)
                p.inc("st", p.sync and p.sync.dma_start(out=dt["dbg_pt"][:, :], in_=pts[pp][:, :]), 16)
            p.wait("tensor", "ac", vptev)
            ins = None
            for j in range(5):
                tj = ti - 4 + j
                ins = p.tensor and p.tensor.matmul(
                    pair_f[:, 640:768],
                    sb["v_sb"][:, tj * 512 + kv * 128: tj * 512 + kv * 128 + 128],
                    pts[pp][:, j * 128:(j + 1) * 128], start=(j == 0), stop=(j == 4))
            vav = p.inc("pe", ins)
            p.wait("scalar", "pe", vav)
            if h == 0 and qt >= 2:
                p.wait("scalar", "st", b1[qt - 2]["av_st"])
            vavev = p.inc("ac", p.scalar and p.scalar.copy(
                avs[qt % 2][:, h * 128:(h + 1) * 128], pair_f[:, 640:768]))
            hs[(qt, h)] = dict(avev=vavev, ptev=vptev, exp=vexp, pn=vpn, tr=vtr,
                               av=vav, nm=vnm)
        q_done = p.cnt["pe"]
        p.wait("sync", "ac", hs[(qt, H - 1)]["avev"])
        vst = p.inc("st", p.sync and p.sync.dma_start(
            out=dt["av_d"][qt * 128:(qt + 1) * 128, :], in_=avs[qt % 2][:, :]), 16)
        if DBG:
            vst = p.inc("st", p.sync and p.sync.dma_start(
                out=dt["dbg_av"][qt * 128:(qt + 1) * 128, :], in_=avs[qt % 2][:, :]), 16)
        b1[qt] = dict(xn=vxn, trx=vtrx, xnT_ev=vxnT, rope_rd=vrope_rd,
                      q_done=q_done, av_st=vst, qp=vqp)
    b1_end = p.snap()

    p.wait("gpsimd", "pe", b1_end["pe"])
    wo3 = dt["wo"].rearrange("(c p) n -> p c n", p=128)
    p.inc("cast", p.gpsimd and p.gpsimd.dma_start(
        out=sb["wo_sb"][:, :].rearrange("p (c n) -> p c n", c=FCH), in_=wo3), 16)
    wo_ready = p.cnt["cast"]

    avbr = [sb["avb0_sb"], sb["avb1_sb"]]
    b2 = {}
    for qt in range(NQ):
        ti = qt + 4
        slot = qt % 2
        p.wait("sync", "st", b1[qt]["av_st"])
        if qt >= 2:
            p.wait("sync", "pe", b2[qt - 2]["wo_done"])
        vav = p.inc("ld", p.sync and p.sync.dma_start(
            out=avbr[slot][:, :], in_=dt["av_d"][qt * 128:(qt + 1) * 128, :]), 16)
        if qt >= 2:
            p.wait("sync", "dv", b2[qt - 2]["z_done"])
        else:
            p.wait("sync", "dv", b1_end["dv"])
        vx = p.inc("ld", p.sync and p.sync.dma_start(
            out=xring[slot][:, :], in_=x3[ti]), 16)
        p.wait("tensor", "ld", vav)
        p.wait("tensor", "cast", wo_ready)
        if qt >= 1:
            p.wait("tensor", "dv", b2[qt - 1]["z_done"])
        ins = None
        for h in range(H):
            for ng in range(4):
                tgt = psf[0] if ng < 2 else psf[1]
                ins = p.tensor and p.tensor.matmul(
                    tgt[:, (ng % 2) * 512:(ng % 2) * 512 + 512],
                    avbr[slot][:, h * 128:(h + 1) * 128],
                    sb["wo_sb"][:, h * D + ng * 512: h * D + ng * 512 + 512],
                    start=(h == 0), stop=(h == H - 1))
        vwo = p.inc("pe", ins)
        p.wait("vector", "pe", vwo)
        p.wait("vector", "ld", vx)
        vz = None
        for ng in range(4):
            src = psf[0] if ng < 2 else psf[1]
            vz = p.inc("dv", p.vector and p.vector.scalar_tensor_tensor(
                sb["z_sb"][:, ng * 512:(ng + 1) * 512],
                src[:, (ng % 2) * 512:(ng % 2) * 512 + 512],
                1.0, xring[slot][:, ng * 512:(ng + 1) * 512], ALU.mult, ALU.add,
                accum_out=sb["zp_sb"][:, ng:ng + 1]))
        p.wait("vector", "dv", vz)
        vzs = p.inc("dv", p.vector and p.vector.tensor_reduce(
            sb["st1_sb"][:, 0:1], sb["zp_sb"][:, 0:4], AX.X, ALU.add))
        p.wait("vector", "dv", vzs)
        vzm = p.inc("dv", p.vector and p.vector.tensor_scalar(
            sb["st1_sb"][:, 1:2], sb["st1_sb"][:, 0:1], 1.0 / D, None, ALU.mult))
        p.wait("scalar", "dv", vz)
        vsq = p.inc("ac", p.scalar and p.scalar.activation(
            sb["sq_sb"][:, :], sb["z_sb"][:, :], AF.Square, accum_out=sb["st1_sb"][:, 2:3]))
        p.wait("vector", "ac", vsq)
        p.wait("vector", "dv", vzm)
        vm2 = p.inc("dv", p.vector and p.vector.tensor_mul(
            sb["st1_sb"][:, 3:4], sb["st1_sb"][:, 1:2], sb["st1_sb"][:, 1:2]))
        p.wait("vector", "dv", vm2)
        vv1 = p.inc("dv", p.vector and p.vector.tensor_scalar(
            sb["st1_sb"][:, 2:3], sb["st1_sb"][:, 2:3], 1.0 / D, None, ALU.mult))
        p.wait("vector", "dv", vv1)
        vvar = p.inc("dv", p.vector and p.vector.tensor_sub(
            sb["st1_sb"][:, 2:3], sb["st1_sb"][:, 2:3], sb["st1_sb"][:, 3:4]))
        p.wait("scalar", "dv", vvar)
        p.scalar and p.scalar.activation(
            sb["st1_sb"][:, 2:3], sb["st1_sb"][:, 2:3], AF.Ln,
            bias=sb["cst_sb"][:, 1:2], scale=1.0)
        vrstd = p.inc("ac", p.scalar and p.scalar.activation(
            sb["st1_sb"][:, 3:4], sb["st1_sb"][:, 2:3], AF.Exp, scale=-0.5))
        p.wait("vector", "ac", vrstd)
        vy = p.inc("dv", p.vector and p.vector.tensor_scalar(
            sb["y_sb"][:, :], sb["z_sb"][:, :], sb["st1_sb"][:, 1:2], sb["st1_sb"][:, 3:4],
            ALU.subtract, ALU.mult))
        p.wait("vector", "dv", vy)
        vyb = p.inc("dv", p.vector and p.vector.tensor_copy(
            sb["ybf_sb"][:, :], sb["y_sb"][:, :]))
        p.wait("tensor", "dv", vyb)
        if qt >= 1:
            p.wait("tensor", "dv", b2[qt - 1]["ytt_ev"])
        ins = None
        for fch in range(FCH):
            ins = p.tensor and p.tensor.transpose(
                psb[2][:, fch * 128:(fch + 1) * 128],
                sb["ybf_sb"][:, fch * 128:(fch + 1) * 128], sb["ident_sb"][:, :])
        vtry = p.inc("pe", ins)
        p.wait("vector", "pe", vtry)
        if qt >= 2:
            p.wait("vector", "st", b2[qt - 2]["ytt_st"])
        vytt = p.inc("dv", p.vector and p.vector.tensor_copy(
            sb["ytt_sb"][:, :], psb[2][:, 0:D]))
        p.wait("sync", "dv", vy)
        vys = p.inc("st", p.sync and p.sync.dma_start(
            out=dt["y_d"][qt * 128:(qt + 1) * 128, :], in_=sb["y_sb"][:, :]), 16)
        if DBG:
            vys = p.inc("st", p.sync and p.sync.dma_start(
                out=dt["dbg_y"][qt * 128:(qt + 1) * 128, :], in_=sb["y_sb"][:, :]), 16)
        p.wait("sync", "dv", vytt)
        yt3 = dt["yt_d"].rearrange("(c p) t -> p c t", p=128)[:, :, qt * 128:(qt + 1) * 128]
        vyts = p.inc("st", p.sync and p.sync.dma_start(
            out=yt3, in_=sb["ytt_sb"][:, :].rearrange("p (c t) -> p c t", c=FCH)), 16)
        b2[qt] = dict(wo_done=vwo, z_done=vy, ytt_ev=vytt, ytt_st=vyts, y_st=vys)
    b2_end = p.snap()

    for e in P.ENGS:
        p.wait(e, "pe", b2_end["pe"])
        p.wait(e, "dv", b2_end["dv"])
        p.wait(e, "ac", b2_end["ac"])
        p.wait(e, "st", b2_end["st"])

    w1r = [sb[f"w1r{i}_sb"] for i in range(3)]
    w2r = [sb[f"w2r{i}_sb"] for i in range(3)]
    ofr = [sb["of0_sb"], sb["of1_sb"]]
    ffn = {}
    for q4 in range(4):
        if q4 >= 1:
            p.wait("sync", "pe", ffn[(q4 - 1, "m1_last")])
            p.wait("sync", "dv", ffn[(q4 - 1, "ln_done")])
        yt_src = dt["yt_d"].rearrange("(c p) t -> p c t", p=128)[:, :, q4 * 512:(q4 + 1) * 512]
        vyt = p.inc("ld", p.sync and p.sync.dma_start(
            out=sb["yT_sb"][:, :].rearrange("p (c t) -> p c t", c=FCH), in_=yt_src), 16)
        y_src = dt["y_d"].rearrange("(t p) d -> p t d", p=128)[:, q4 * 4:(q4 + 1) * 4, :]
        vo2 = p.inc("ld", p.sync and p.sync.dma_start(
            out=sb["out2_sb"][:, :].rearrange("p (t d) -> p t d", t=4), in_=y_src), 16)
        p.wait("vector", "ld", vo2)
        o2v = sb["out2_sb"][:, :].rearrange("p (t d) -> p t d", t=4)
        b2v = sb["b2r_sb"][:, :].unsqueeze(1).broadcast_to([128, 4, D])
        vb2 = p.inc("dv", p.vector and p.vector.tensor_tensor(o2v, o2v, b2v, ALU.add))
        vm1 = None
        for fc in range(NFC):
            slot = fc % 3
            if fc >= 3:
                prevk = (q4, fc - 3)
            elif q4 > 0:
                prevk = (q4 - 1, [63, 61, 62][fc])
            else:
                prevk = None
            if prevk is not None and (prevk[0], "m1", prevk[1]) in ffn:
                p.wait("sync", "pe", ffn[(prevk[0], "m1", prevk[1])])
            vw1 = p.inc("ld", p.sync and p.sync.dma_start(
                out=w1r[slot][:, :], in_=dt["w1q_d"][fc * 128:(fc + 1) * 128, :]), 16)
            p.wait("tensor", "ld", vw1)
            p.wait("tensor", "ld", vyt)
            if q4 == 0 and fc == 0:
                p.wait("tensor", "st", wq_done_st)
            hps = psf[fc % 2][:, 0:512]
            gp = ffn.get((q4, "gelu", fc - 2)) or (ffn.get((q4 - 1, "gelu", NFC - 2 + fc)) if q4 > 0 else None)
            if gp:
                p.wait("tensor", "ac", gp)
            if q4 > 0 and fc < 2:
                p.wait("tensor", "dv", ffn[(q4 - 1, "flush_last")])
            ins = None
            for fch in range(FCH):
                ins = p.tensor and p.tensor.matmul(
                    hps, w1r[slot][:, fch * 128:(fch + 1) * 128],
                    sb["yT_sb"][:, fch * 512:(fch + 1) * 512],
                    start=(fch == 0), stop=(fch == FCH - 1))
            vm1 = p.inc("pe", ins)
            ffn[(q4, "m1", fc)] = vm1
            p.wait("scalar", "pe", vm1)
            pm2 = ffn.get((q4 - 1, "m2_last")) if q4 > 0 and fc < 2 else None
            if pm2:
                p.wait("scalar", "pe", pm2)
            vgel = p.inc("ac", p.scalar and p.scalar.activation(
                sb["h_sb"][:, fc * 512:(fc + 1) * 512], hps, AF.Gelu,
                bias=sb["b1_sb"][:, fc:fc + 1], scale=0.5))
            ffn[(q4, "gelu", fc)] = vgel
        ffn[(q4, "m1_last")] = vm1
        gelu_all = p.cnt["ac"]
        vfl = None
        vm2 = None
        w2q4 = dt["w2q_d"].rearrange("(g p) d -> p g d", p=128)
        for ng in range(4):
            pool = [psf[2], psf[3]] if ng % 2 == 0 else [psf[0], psf[1]]
            for fc in range(NFC):
                grp = fc // 4
                slot3 = grp % 3
                if fc % 4 == 0:
                    if grp >= 3:
                        prevu = ffn.get((q4, ng, grp - 3))
                    else:
                        pg = [15, 13, 14][grp]
                        prevu = (ffn.get((q4, ng - 1, pg)) if ng > 0 else
                                 (ffn.get((q4 - 1, 3, pg)) if q4 > 0 else None))
                    if prevu is not None:
                        p.wait("sync", "pe", prevu)
                    vw2 = p.inc("ld", p.sync and p.sync.dma_start(
                        out=w2r[slot3][:, :].rearrange("p (g c) -> p g c", g=4),
                        in_=w2q4[:, fc:fc + 4, ng * 512:(ng + 1) * 512]), 16)
                p.wait("tensor", "ld", vw2)
                p.wait("tensor", "ac", ffn[(q4, "gelu", fc)])
                if ng % 2 == 1 and fc == 0:
                    p.wait("tensor", "ac", gelu_all)
                if fc == 0:
                    fp = ffn.get((q4, "flush", ng - 2))
                    if fp:
                        p.wait("tensor", "dv", fp)
                ins = None
                for tt in range(4):
                    tgt = pool[tt // 2]
                    ins = p.tensor and p.tensor.matmul(
                        tgt[:, (tt % 2) * 512:(tt % 2) * 512 + 512],
                        sb["h_sb"][:, fc * 512 + tt * 128: fc * 512 + tt * 128 + 128],
                        w2r[slot3][:, (fc % 4) * 512:(fc % 4) * 512 + 512],
                        start=(fc == 0), stop=(fc == NFC - 1))
                vm2 = p.inc("pe", ins)
                if fc % 4 == 3 or fc == NFC - 1:
                    ffn[(q4, ng, grp)] = vm2
            ffn[(q4, "m2", ng)] = vm2
            p.wait("vector", "pe", vm2)
            p.wait("vector", "dv", vb2)
            for tt in range(4):
                src = pool[tt // 2]
                ins = p.vector and p.vector.scalar_tensor_tensor(
                    sb["out2_sb"][:, tt * D + ng * 512: tt * D + ng * 512 + 512],
                    src[:, (tt % 2) * 512:(tt % 2) * 512 + 512], 0.5,
                    sb["out2_sb"][:, tt * D + ng * 512: tt * D + ng * 512 + 512],
                    ALU.mult, ALU.add)
                vfl = p.inc("dv", ins)
            ffn[(q4, "flush", ng)] = vfl
        ffn[(q4, "m2_last")] = vm2
        ffn[(q4, "flush_last")] = vfl
        if DBG:
            p.wait("sync", "dv", vfl)
            p.inc("st", p.sync and p.sync.dma_start(
                out=dt["dbg_o2"].rearrange("(t p) d -> p t d", p=128)[:, q4 * 4:(q4 + 1) * 4, :],
                in_=sb["out2_sb"][:, :].rearrange("p (t d) -> p t d", t=4)), 16)
        vln = None
        for tt in range(4):
            p.wait("scalar", "dv", ffn[(q4, "flush", 3)])
            vsum = p.inc("ac", p.scalar and p.scalar.activation(
                sb["fsq_sb"][:, :], sb["out2_sb"][:, tt * D:(tt + 1) * D], AF.Copy,
                accum_out=sb["st1_sb"][:, 0:1]))
            vsq = p.inc("ac", p.scalar and p.scalar.activation(
                sb["fsq_sb"][:, :], sb["out2_sb"][:, tt * D:(tt + 1) * D], AF.Square,
                accum_out=sb["st1_sb"][:, 2:3]))
            p.wait("vector", "ac", vsq)
            vzm = p.inc("dv", p.vector and p.vector.tensor_scalar(
                sb["st1_sb"][:, 1:2], sb["st1_sb"][:, 0:1], 1.0 / D, None, ALU.mult))
            p.wait("vector", "dv", vzm)
            vm2_ = p.inc("dv", p.vector and p.vector.tensor_mul(
                sb["st1_sb"][:, 3:4], sb["st1_sb"][:, 1:2], sb["st1_sb"][:, 1:2]))
            p.wait("vector", "dv", vm2_)
            vv1 = p.inc("dv", p.vector and p.vector.tensor_scalar(
                sb["st1_sb"][:, 2:3], sb["st1_sb"][:, 2:3], 1.0 / D, None, ALU.mult))
            p.wait("vector", "dv", vv1)
            vvar = p.inc("dv", p.vector and p.vector.tensor_sub(
                sb["st1_sb"][:, 2:3], sb["st1_sb"][:, 2:3], sb["st1_sb"][:, 3:4]))
            p.wait("scalar", "dv", vvar)
            p.scalar and p.scalar.activation(
                sb["st1_sb"][:, 2:3], sb["st1_sb"][:, 2:3], AF.Ln,
                bias=sb["cst_sb"][:, 1:2], scale=1.0)
            vrstd = p.inc("ac", p.scalar and p.scalar.activation(
                sb["st1_sb"][:, 3:4], sb["st1_sb"][:, 2:3], AF.Exp, scale=-0.5))
            p.wait("vector", "ac", vrstd)
            pst = ffn.get((q4, "ofst", tt - 2))
            if pst:
                p.wait("vector", "st", pst)
            vout = p.inc("dv", p.vector and p.vector.tensor_scalar(
                ofr[tt % 2][:, :], sb["out2_sb"][:, tt * D:(tt + 1) * D],
                sb["st1_sb"][:, 1:2], sb["st1_sb"][:, 3:4], ALU.subtract, ALU.mult))
            vln = vout
            p.wait("sync", "dv", vout)
            row0 = (q4 * 4 + tt) * 128
            vst = p.inc("st", p.sync and p.sync.dma_start(
                out=dt["out_sh"][row0:row0 + 128, :], in_=ofr[tt % 2][:, :]), 16)
            ffn[(q4, "ofst", tt)] = vst
        ffn[(q4, "ln_done")] = vln

    for e in P.ENGS:
        p.wait(e, "st")


_NC_CACHE = {}


def kernel(**inputs):
    in_maps = _stage_inputs(**inputs)
    if "main" not in _NC_CACHE:
        _NC_CACHE["main"] = build_nc()
    nc = _NC_CACHE["main"]
    res = run_bass_kernel_spmd(nc, in_maps, core_ids=list(range(8)))
    outs = [r["out_sh"] for r in res.results]
    full = np.concatenate(outs, axis=0).reshape(4, 4096, D).astype(np.float32)
    if DBG:
        kernel._dbg = res.results
    return full


# version 04
# speedup vs baseline: 1.0032x; 1.0032x over previous
import os
import numpy as np
import ml_dtypes
import concourse.bass as bass
import concourse.mybir as mybir
from concourse.bass_utils import run_bass_kernel_spmd

F32 = mybir.dt.float32
BF16 = mybir.dt.bfloat16
AF = mybir.ActivationFunctionType
ALU = mybir.AluOpType
AX = mybir.AxisListType
bf16 = ml_dtypes.bfloat16

D = 2048
H = 16
KH = 4
HD = 128
WIN = 512
T = 2048
TH = 2560
NT = 20
NQ = 16
FF = 8192
NFC = 64
FCH = 16
SC = 0.08838834764831845
RMS_EPS = 1e-6
LN_EPS = 1e-5
Q_EPS = 1e-8
KB = 1024
DBG = bool(int(os.environ.get("ATH_DBG", "0")))


def _host_constants(half):
    start = half * T - WIN
    pos = (np.arange(TH, dtype=np.float64) + start).astype(np.float32)
    inv_freq = (1.0 / 10000.0 ** (np.arange(0, HD, 2, dtype=np.float32) / HD)).astype(np.float32)
    ang = (pos[:, None] * inv_freq[None, :]).astype(np.float32)
    cos = np.cos(ang.astype(np.float64)).astype(np.float32)
    sin = np.sin(ang.astype(np.float64)).astype(np.float32)
    c2 = np.zeros((128, NT * 128), np.float32)
    s2 = np.zeros((128, NT * 128), np.float32)
    for ti in range(NT):
        cs = cos[ti * 128:(ti + 1) * 128]
        sn = sin[ti * 128:(ti + 1) * 128]
        c2[:, ti * 128:(ti + 1) * 128:2] = cs
    c2[:] = 0; s2[:] = 0
    for ti in range(NT):
        cs = cos[ti * 128:(ti + 1) * 128]
        sn = sin[ti * 128:(ti + 1) * 128]
        blk_c = np.empty((128, 128), np.float32)
        blk_s = np.empty((128, 128), np.float32)
        blk_c[:, 0::2] = cs
        blk_c[:, 1::2] = cs
        blk_s[:, 0::2] = -sn
        blk_s[:, 1::2] = sn
        c2[:, ti * 128:(ti + 1) * 128] = blk_c
        s2[:, ti * 128:(ti + 1) * 128] = blk_s
    p = np.arange(128)[:, None]
    c = np.arange(640)[None, :]
    base = np.where((c >= p) & (c <= p + WIN), 0.0, -1e30).astype(np.float32)
    masks = np.zeros((128, 5 * 640), np.float32)
    for cls in range(5):
        m = base.copy()
        if half == 0 and cls < 4:
            m = m + np.where(c >= WIN - cls * 128, 0.0, -1e30)
        masks[:, cls * 640:(cls + 1) * 640] = np.maximum(m, -1e30)
    return c2.astype(bf16), s2.astype(bf16), masks.astype(bf16)


def _stage_inputs(L0_attn_x, L0_attn_ln_w, L0_attn_wq, L0_attn_wk, L0_attn_wv,
                  L0_attn_wo, L0_ffn_w1, L0_ffn_b1, L0_ffn_w2, L0_ffn_b2):
    xf = np.ascontiguousarray(np.asarray(L0_attn_x, np.float32).reshape(4 * 4096, D))
    w1 = np.asarray(L0_ffn_w1, np.float32)
    w1s = np.ascontiguousarray(
        w1.reshape(NFC, 128, FCH, 128).transpose(0, 3, 2, 1).reshape(FF, D))
    w2t = np.ascontiguousarray(np.asarray(L0_ffn_w2, np.float32).T)
    lnw_rep = np.ascontiguousarray(
        np.broadcast_to(np.asarray(L0_attn_ln_w, np.float32)[None, :], (128, D)))
    b2_rep = np.ascontiguousarray(
        np.broadcast_to(np.asarray(L0_ffn_b2, np.float32)[None, :], (128, D)))
    b1s = np.ascontiguousarray(np.asarray(L0_ffn_b1, np.float32).reshape(NFC, 128).T)
    ident = np.eye(128, dtype=bf16)
    onesr = np.ones((1, 128), np.float32)
    csts = np.zeros((128, 8), np.float32)
    csts[:, 0] = RMS_EPS
    csts[:, 1] = LN_EPS
    csts[:, 2] = Q_EPS
    csts[:, 3] = 1.0

    shared = dict(
        wq=np.ascontiguousarray(np.asarray(L0_attn_wq, np.float32)),
        wk=np.ascontiguousarray(np.asarray(L0_attn_wk, np.float32)),
        wv=np.ascontiguousarray(np.asarray(L0_attn_wv, np.float32)),
        wo=np.ascontiguousarray(np.asarray(L0_attn_wo, np.float32)),
        w1s=w1s, w2t=w2t, lnw_rep=lnw_rep, b2_rep=b2_rep, b1s=b1s,
        ident=ident, onesr=onesr, csts=csts,
    )
    in_maps = []
    for core in range(8):
        b, half = core >> 1, core & 1
        rows0 = b * 4096 + half * T
        x_sh = np.zeros((TH, D), np.float32)
        if half == 0:
            x_sh[WIN:] = xf[rows0:rows0 + T]
        else:
            x_sh[:] = xf[rows0 - WIN:rows0 + T]
        c2, s2, masks = _host_constants(half)
        m = dict(shared)
        m.update(x_sh=np.ascontiguousarray(x_sh), c2t=np.ascontiguousarray(c2),
                 s2t=np.ascontiguousarray(s2), masks=np.ascontiguousarray(masks))
        in_maps.append(m)
    return in_maps


class P:
    ENGS = ("sync", "gpsimd", "tensor", "vector", "scalar")

    def __init__(self, nc, sems, engs, cur):
        self.nc = nc
        self.cur = cur
        for e in P.ENGS:
            setattr(self, e, engs[e] if e == cur else None)
        self.sems = sems
        self.cnt = {k: 0 for k in sems}
        self._lastw = {}

    def inc(self, sem, instr, k=1):
        self.cnt[sem] += k
        if instr is not None:
            instr.then_inc(self.sems[sem], k)
        return self.cnt[sem]

    def wait(self, eng, sem, val=None):
        v = self.cnt[sem] if val is None else val
        if v is None or v <= 0:
            return
        if getattr(self, eng) is not None and self._lastw.get((eng, sem), -1) < v:
            getattr(self, eng).wait_ge(self.sems[sem], v)
            self._lastw[(eng, sem)] = v

    def snap(self):
        return dict(self.cnt)


def build_nc():
    nc = bass.Bass("TRN2", target_bir_lowering=False, detect_race_conditions=False)
    dt = {}

    def din(name, shape, dtype=F32):
        dt[name] = nc.dram_tensor(name, list(shape), dtype, kind="ExternalInput").ap()

    def dout(name, shape, dtype=F32):
        dt[name] = nc.dram_tensor(name, list(shape), dtype, kind="ExternalOutput").ap()

    def dint(name, shape, dtype):
        dt[name] = nc.dram_tensor(name, list(shape), dtype, kind="Internal").ap()

    din("x_sh", (TH, D))
    din("wq", (D, D)); din("wk", (D, 512)); din("wv", (D, 512)); din("wo", (D, D))
    din("w1s", (FF, D)); din("w2t", (FF, D))
    din("lnw_rep", (128, D)); din("b2_rep", (128, D)); din("b1s", (128, NFC))
    din("ident", (128, 128), BF16); din("onesr", (1, 128)); din("csts", (128, 8))
    din("c2t", (128, NT * 128), BF16); din("s2t", (128, NT * 128), BF16)
    din("masks", (128, 5 * 640), BF16)
    dout("out_sh", (T, D))
    if DBG:
        dout("dbg_y", (T, D))
        dout("dbg_av", (T, D), BF16)
        dout("dbg_kt", (128, NT * 512), BF16)
        dout("dbg_v", (128, NT * 512), BF16)
        dout("dbg_qt", (NQ * 128, D), BF16)
        dout("dbg_pn", (128, 640), BF16)
        dout("dbg_pt", (128, 640), BF16)
        dout("dbg_sc", (128, 640))
        dout("dbg_o2", (T, D))
    dint("y_d", (T, D), F32)
    dint("yt_d", (D, T), BF16)
    dint("av_d", (T, D), BF16)
    dint("w1q_d", (FF, D), BF16)
    dint("w2q_d", (FF, D), BF16)

    sb = {}
    LIMIT = 224 * KB - 64

    def S(name, shape, dtype, off):
        n = int(np.prod(shape[1:])) * (2 if dtype == BF16 else 4)
        assert off + n <= LIMIT, (name, off, n)
        sb[name] = nc.alloc_sbuf_tensor_at(name, list(shape), dtype, offset=off)
        return off + ((n + 31) & ~31)

    o = 17 * KB
    o = S("ident_sb", (128, 128), BF16, o)
    o = S("b1_sb", (128, NFC), F32, o)
    o = S("cst_sb", (128, 8), F32, o)
    o = S("onesr_sb", (1, 128), F32, o)
    o = S("rstd_sb", (128, NT), F32, o)
    o = S("th_sb", (128, 4), F32, o)
    o = S("sum1_sb", (128, NFC), F32, o)
    o = S("red_sb", (128, 2), F32, o)
    o = S("tot_sb", (1, 2), F32, o)
    o = S("sval_sb", (128, 2), F32, o)
    o = S("b2r_sb", (128, D), F32, o)
    o = S("zp_sb", (128, 8), F32, o)
    o = S("nm2_sb", (128, 4), F32, o)
    o = S("nm1_sb", (128, 2), F32, o)
    o = S("nms_sb", (128, 2), F32, o)
    o = S("rows_sb", (128, 2), F32, o)
    o = S("rec_sb", (128, 2), F32, o)
    o = S("st1_sb", (128, 4), F32, o)
    SBASE = o

    o = S("kT_sb", (128, NT * 512), BF16, o)
    o = S("v_sb", (128, NT * 512), BF16, o)
    o = S("c2_sb", (128, NT * 128), BF16, o)
    o = S("s2_sb", (128, NT * 128), BF16, o)
    o = S("masks_sb", (128, 5 * 640), BF16, o)
    o = S("lnw_sb", (128, D), F32, o)
    o_x0 = o
    o = S("x0_sb", (128, D), F32, o)
    o = S("x1_sb", (128, D), F32, o)
    o_xn = o
    o = S("xna_sb", (128, D), BF16, o)
    o = S("xnb_sb", (128, D), BF16, o)
    o = S("xnT_sb", (128, D), BF16, o)
    o_sq = o
    o = S("sq_sb", (128, D), F32, o)
    o_t1 = o
    o = S("t1_sb", (128, 1024), F32, o)
    o = S("t2_sb", (128, 1024), F32, o)
    o_kr = o
    o = S("krot_sb", (128, 512), BF16, o)
    WSLOT = o
    o = S("wq_sb", (128, FCH * D), BF16, o)
    o_q = o
    o = S("q_sb", (128, D), BF16, o)
    o = S("qT_sb", (128, D), BF16, o)
    o_pr = o
    o = S("pr0_sb", (128, 640), BF16, o)
    o = S("pr1_sb", (128, 640), BF16, o)
    o_pn = o
    o = S("pn0_sb", (128, 640), BF16, o)
    o = S("pn1_sb", (128, 640), BF16, o)
    o = S("pt0_sb", (128, 640), BF16, o)
    o = S("pt1_sb", (128, 640), BF16, o)
    o_av = o
    o = S("av0_sb", (128, D), BF16, o)
    o = S("av1_sb", (128, D), BF16, o)
    ATT_END = o

    o2 = WSLOT
    o2 = S("wk_sb", (128, FCH * 512), BF16, o2)
    o2 = S("wv_sb", (128, FCH * 512), BF16, o2)
    o2 = S("ws0_sb", (128, D), F32, o2)
    o2 = S("ws1_sb", (128, D), F32, o2)
    o2 = S("sga_sb", (128, D), BF16, o2)
    o2 = S("sgb_sb", (128, D), BF16, o2)
    o2 = S("q2_sb", (128, D), BF16, o2)
    assert o2 <= WSLOT + FCH * D * 2, o2
    S("wo_sb", (128, FCH * D), BF16, WSLOT)
    kT_off = SBASE
    S("ws0b_sb", (128, D), F32, kT_off)
    S("ws1b_sb", (128, D), F32, kT_off + 8 * KB)
    S("sgab_sb", (128, D), BF16, kT_off + 20 * KB)
    S("sgbb_sb", (128, D), BF16, kT_off + 20 * KB + 4 * KB)
    S("q2b_sb", (128, D), BF16, kT_off + 20 * KB + 8 * KB)
    S("avb0_sb", (128, D), BF16, o_av)
    S("avb1_sb", (128, D), BF16, o_av + 4 * KB)
    S("z_sb", (128, D), F32, o_t1)
    S("y_sb", (128, D), F32, o_xn)
    S("ybf_sb", (128, D), BF16, o_q)
    S("ytt_sb", (128, D), BF16, o_pn)

    o = SBASE
    o = S("out2_sb", (128, 4 * D), F32, o)
    o = S("yT_sb", (128, FCH * 512), BF16, o)
    o = S("h_sb", (128, NFC * 512), BF16, o)
    for i in range(3):
        o = S(f"w1r{i}_sb", (128, D), BF16, o)
    for i in range(3):
        o = S(f"w2r{i}_sb", (128, 4 * 512), BF16, o)
    o = S("fsq_sb", (128, D), F32, o)
    o = S("of0_sb", (128, D), F32, o)
    o = S("of1_sb", (128, D), F32, o)
    assert o <= LIMIT and ATT_END <= LIMIT, (o, ATT_END)

    ps = [nc.alloc_psum_tensor(f"P{i}", [128, 1024], F32) for i in range(4)]
    sems = {n: nc.alloc_semaphore(name=n) for n in
            ["ld", "cast", "st", "pe", "dv", "ac", "bar"]}
    engs_map = dict(sync=nc.sync, gpsimd=nc.gpsimd, tensor=nc.tensor,
                    vector=nc.vector, scalar=nc.scalar)

    with nc.Block() as block:
        @block.sync
        def _(eng):
            emit_all(P(nc, sems, engs_map, "sync"), dt, sb, ps)

        @block.gpsimd
        def _(eng):
            emit_all(P(nc, sems, engs_map, "gpsimd"), dt, sb, ps)

        @block.tensor
        def _(eng):
            emit_all(P(nc, sems, engs_map, "tensor"), dt, sb, ps)

        @block.vector
        def _(eng):
            emit_all(P(nc, sems, engs_map, "vector"), dt, sb, ps)

        @block.scalar
        def _(eng):
            emit_all(P(nc, sems, engs_map, "scalar"), dt, sb, ps)

    return nc


def emit_all(p, dt, sb, ps):
    psb = [t[:, :].bitcast(BF16) for t in ps]
    psf = [t[:, :] for t in ps]

    for (dst, src) in [("ident_sb", "ident"), ("b1_sb", "b1s"), ("cst_sb", "csts"),
                       ("onesr_sb", "onesr"), ("b2r_sb", "b2_rep"),
                       ("c2_sb", "c2t"), ("s2_sb", "s2t"), ("masks_sb", "masks"),
                       ("lnw_sb", "lnw_rep")]:
        p.inc("ld", p.sync and p.sync.dma_start(out=sb[dst][:, :], in_=dt[src][:, :]), 16)
    lds_consts = p.cnt["ld"]

    wk3 = dt["wk"].rearrange("(c p) n -> p c n", p=128)
    p.inc("cast", p.gpsimd and p.gpsimd.dma_start(
        out=sb["wk_sb"][:, :].rearrange("p (c n) -> p c n", c=FCH), in_=wk3), 16)
    wv3 = dt["wv"].rearrange("(c p) n -> p c n", p=128)
    p.inc("cast", p.gpsimd and p.gpsimd.dma_start(
        out=sb["wv_sb"][:, :].rearrange("p (c n) -> p c n", c=FCH), in_=wv3), 16)
    kv_w_ready = p.cnt["cast"]

    wsrc = ["w1s", "w2t"]
    wdst = ["w1q_d", "w2q_d"]
    wst = {"red": {}, "q": {}, "th": [0, 0]}

    def scale_unit(wi, s, bufs, scratch):
        ring = bufs[0:2]
        slot = s % 2
        prevk = (wi, s - 2)
        if prevk in wst["red"]:
            p.wait("sync", "dv", wst["red"][prevk])
        v = p.inc("ld", p.sync and p.sync.dma_start(
            out=ring[slot][:, :], in_=dt[wsrc[wi]][s * 128:(s + 1) * 128, :]), 16)
        p.wait("vector", "ld", v)
        wst["red"][(wi, s)] = p.inc("dv", p.vector and p.vector.reduce_sum(
            sb["sum1_sb"][:, s:s + 1], ring[slot][:, :], axis=AX.X,
            apply_absolute_value=True))

    def th_calc(wi):
        p.wait("vector", "dv", wst["red"][(wi, NFC - 1)])
        v = p.inc("dv", p.vector and p.vector.tensor_reduce(
            sb["red_sb"][:, wi:wi + 1], sb["sum1_sb"][:, 0:NFC], AX.X, ALU.add))
        p.wait("tensor", "dv", v)
        p.wait("tensor", "ld", lds_consts)
        vpe = p.inc("pe", p.tensor and p.tensor.matmul(
            psf[3][0:1, 0:1], sb["cst_sb"][:, 3:4], sb["red_sb"][:, wi:wi + 1],
            start=True, stop=True, skip_group_check=True))
        p.wait("scalar", "pe", vpe)
        vac = p.inc("ac", p.scalar and p.scalar.copy(
            sb["tot_sb"][0:1, wi:wi + 1], psf[3][0:1, 0:1]))
        p.wait("tensor", "ac", vac)
        vpe = p.inc("pe", p.tensor and p.tensor.matmul(
            psf[3][:, 4:5], sb["onesr_sb"][0:1, :], sb["tot_sb"][0:1, wi:wi + 1],
            start=True, stop=True, skip_group_check=True))
        p.wait("vector", "pe", vpe)
        v = p.inc("dv", p.vector and p.vector.tensor_scalar(
            sb["sval_sb"][:, wi:wi + 1], psf[3][:, 4:5], 1.0 / (FF * D), sb["cst_sb"][:, 2:3],
            ALU.mult, ALU.max))
        p.wait("vector", "dv", v)
        v = p.inc("dv", p.vector and p.vector.tensor_scalar(
            sb["th_sb"][:, wi:wi + 1], sb["sval_sb"][:, wi:wi + 1], 0.5, None, ALU.mult))
        p.wait("vector", "dv", v)
        v = p.inc("dv", p.vector and p.vector.tensor_scalar(
            sb["th_sb"][:, 2 + wi:3 + wi], sb["sval_sb"][:, wi:wi + 1], -0.5, None, ALU.mult))
        wst["th"][wi] = v

    def quant_unit(wi, s, bufs):
        ring = bufs[0:2]
        sga, sgb, q2 = bufs[2], bufs[3], bufs[4]
        slot = s % 2
        prevk = (wi, s - 2)
        prev1 = (wi, s - 1)
        if prevk in wst["q"]:
            p.wait("sync", "ac", wst["q"][prevk]["sg"])
        vld = p.inc("ld", p.sync and p.sync.dma_start(
            out=ring[slot][:, :], in_=dt[wsrc[wi]][s * 128:(s + 1) * 128, :]), 16)
        p.wait("scalar", "ld", vld)
        p.wait("scalar", "dv", wst["th"][wi])
        if prev1 in wst["q"]:
            p.wait("scalar", "dv", wst["q"][prev1]["add"])
        p.scalar and p.scalar.activation(
            sga[:, :], ring[slot][:, :], AF.Sign,
            bias=sb["th_sb"][:, 2 + wi:3 + wi], scale=1.0)
        vsg = p.inc("ac", p.scalar and p.scalar.activation(
            sgb[:, :], ring[slot][:, :], AF.Sign,
            bias=sb["th_sb"][:, wi:wi + 1], scale=1.0))
        p.wait("vector", "ac", vsg)
        if prev1 in wst["q"]:
            p.wait("vector", "st", wst["q"][prev1]["st"])
        vadd = p.inc("dv", p.vector and p.vector.tensor_add(
            q2[:, :], sga[:, :], sgb[:, :]))
        p.wait("sync", "dv", vadd)
        vst = p.inc("st", p.sync and p.sync.dma_start(
            out=dt[wdst[wi]][s * 128:(s + 1) * 128, :], in_=q2[:, :]), 16)
        wst["q"][(wi, s)] = dict(sg=vsg, add=vadd, st=vst)

    bufsA = [sb["ws0_sb"], sb["ws1_sb"], sb["sga_sb"], sb["sgb_sb"], sb["q2_sb"]]
    bufsB = [sb["ws0b_sb"], sb["ws1b_sb"], sb["sgab_sb"], sb["sgbb_sb"], sb["q2b_sb"]]

    x3 = dt["x_sh"].rearrange("(t p) d -> t p d", p=128)
    xring = [sb["x0_sb"], sb["x1_sb"]]
    xns = [sb["xna_sb"], sb["xnb_sb"]]

    ast = {}

    def a_stats(ti):
        slot = ti % 2
        if ti >= 2:
            p.wait("sync", "dv", ast[ti - 2]["xn"])
        vx = p.inc("ld", p.sync and p.sync.dma_start(
            out=xring[slot][:, :], in_=x3[ti]), 16)
        p.wait("scalar", "ld", vx)
        p.wait("scalar", "ld", lds_consts)
        p.scalar and p.scalar.activation(
            sb["sq_sb"][:, :], xring[slot][:, :], AF.Square, accum_out=sb["st1_sb"][:, 0:1])
        p.scalar and p.scalar.activation(
            sb["st1_sb"][:, 1:2], sb["st1_sb"][:, 0:1], AF.Ln,
            bias=sb["cst_sb"][:, 0:1], scale=1.0 / D)
        vr = p.inc("ac", p.scalar and p.scalar.activation(
            sb["rstd_sb"][:, ti:ti + 1], sb["st1_sb"][:, 1:2], AF.Exp, scale=-0.5))
        p.wait("vector", "ac", vr)
        p.wait("vector", "ld", lds_consts)
        if ti >= 2:
            p.wait("vector", "pe", ast[ti - 2]["trx"])
        vxn = p.inc("dv", p.vector and p.vector.scalar_tensor_tensor(
            xns[slot][:, :], xring[slot][:, :], sb["rstd_sb"][:, ti:ti + 1],
            sb["lnw_sb"][:, :], ALU.mult, ALU.mult))
        ast[ti] = dict(xn=vxn)

    a_stats(0)
    for ti in range(NT):
        slot = ti % 2
        if ti + 1 < NT:
            a_stats(ti + 1)
        if ti < 8:
            for j in range(8):
                scale_unit(0, ti * 8 + j, bufsA, sb["sq_sb"])
        else:
            u0 = (ti - 8) * 64 // 12
            u1 = (ti - 7) * 64 // 12
            for u in range(u0, u1):
                quant_unit(0, u, bufsA)
        if ti == 7:
            th_calc(0)
        p.wait("tensor", "dv", ast[ti]["xn"])
        if ti >= 1:
            p.wait("tensor", "dv", ast[ti - 1]["xnT_ev"])
        ins = None
        for fch in range(FCH):
            ins = p.tensor and p.tensor.transpose(
                psb[2][:, fch * 128:(fch + 1) * 128],
                xns[slot][:, fch * 128:(fch + 1) * 128], sb["ident_sb"][:, :])
        vtrx = p.inc("pe", ins)
        p.wait("vector", "pe", vtrx)
        if ti >= 1:
            p.wait("vector", "pe", ast[ti - 1]["kv"])
        vxnT = p.inc("dv", p.vector and p.vector.tensor_copy(
            sb["xnT_sb"][:, :], psb[2][:, 0:D]))
        p.wait("tensor", "dv", vxnT)
        p.wait("tensor", "cast", kv_w_ready)
        if ti >= 1:
            p.wait("tensor", "dv", ast[ti - 1]["rope_rd"])
            p.wait("tensor", "ac", ast[ti - 1]["vev"])
        ins = None
        for fch in range(FCH):
            p.tensor and p.tensor.matmul(
                psf[0][:, 0:512], sb["xnT_sb"][:, fch * 128:(fch + 1) * 128],
                sb["wk_sb"][:, fch * 512:(fch + 1) * 512],
                start=(fch == 0), stop=(fch == FCH - 1))
            ins = p.tensor and p.tensor.matmul(
                psf[0][:, 512:1024], sb["xnT_sb"][:, fch * 128:(fch + 1) * 128],
                sb["wv_sb"][:, fch * 512:(fch + 1) * 512],
                start=(fch == 0), stop=(fch == FCH - 1))
        vkv = p.inc("pe", ins)
        p.wait("vector", "pe", vkv)
        kview = psf[0][:, 0:512].rearrange("p (h c) -> p h c", h=KH)
        c2s = sb["c2_sb"][:, ti * 128:(ti + 1) * 128].unsqueeze(1).broadcast_to([128, KH, 128])
        p.vector and p.vector.tensor_tensor(
            sb["t1_sb"][:, 0:512].rearrange("p (h c) -> p h c", h=KH), kview, c2s, ALU.mult)
        kv4 = psf[0][:, 0:512].rearrange("p (h i pr) -> p h i pr", h=KH, pr=2)
        t24 = sb["t2_sb"][:, 0:512].rearrange("p (h i pr) -> p h i pr", h=KH, pr=2)
        s24 = sb["s2_sb"][:, ti * 128:(ti + 1) * 128].rearrange(
            "p (i pr) -> p i pr", pr=2).unsqueeze(1).broadcast_to([128, KH, 64, 2])
        p.vector and p.vector.tensor_tensor(
            t24[:, :, :, 0:1], kv4[:, :, :, 1:2], s24[:, :, :, 0:1], ALU.mult)
        vrope_rd = p.inc("dv", p.vector and p.vector.tensor_tensor(
            t24[:, :, :, 1:2], kv4[:, :, :, 0:1], s24[:, :, :, 1:2], ALU.mult))
        p.wait("vector", "dv", vrope_rd)
        if ti >= 1:
            p.wait("vector", "pe", ast[ti - 1]["trk"])
        vkr = p.inc("dv", p.vector and p.vector.tensor_add(
            sb["krot_sb"][:, :], sb["t1_sb"][:, 0:512], sb["t2_sb"][:, 0:512]))
        p.wait("tensor", "dv", vkr)
        if ti >= 1:
            p.wait("tensor", "dv", ast[ti - 1]["ktr_ev"])
        ins = None
        for kh in range(KH):
            ins = p.tensor and p.tensor.transpose(
                psb[1][:, kh * 128:(kh + 1) * 128],
                sb["krot_sb"][:, kh * 128:(kh + 1) * 128], sb["ident_sb"][:, :])
        vtrk = p.inc("pe", ins)
        p.wait("vector", "pe", vtrk)
        vktr = p.inc("dv", p.vector and p.vector.tensor_copy(
            sb["kT_sb"][:, ti * 512:(ti + 1) * 512], psb[1][:, 0:512]))
        p.wait("scalar", "pe", vkv)
        vvev = p.inc("ac", p.scalar and p.scalar.copy(
            sb["v_sb"][:, ti * 512:(ti + 1) * 512], psf[0][:, 512:1024]))
        ast[ti].update(trx=vtrx, xnT_ev=vxnT, rope_rd=vrope_rd,
                       vev=vvev, trk=vtrk, ktr_ev=vktr, kv=vkv)
    a_end = p.snap()
    w1q_st_done = wst["q"][(0, NFC - 1)]["st"]

    p.wait("gpsimd", "pe", a_end["pe"])
    p.wait("gpsimd", "dv", a_end["dv"])
    p.wait("gpsimd", "ac", a_end["ac"])
    p.wait("gpsimd", "st", a_end["st"])
    wq3 = dt["wq"].rearrange("(c p) n -> p c n", p=128)
    p.inc("cast", p.gpsimd and p.gpsimd.dma_start(
        out=sb["wq_sb"][:, :].rearrange("p (c n) -> p c n", c=FCH), in_=wq3), 16)
    wq_ready = p.cnt["cast"]

    if DBG:
        p.wait("sync", "dv", a_end["dv"])
        p.wait("sync", "ac", a_end["ac"])
        p.inc("st", p.sync and p.sync.dma_start(out=dt["dbg_kt"][:, :], in_=sb["kT_sb"][:, :]), 16)
        p.inc("st", p.sync and p.sync.dma_start(out=dt["dbg_v"][:, :], in_=sb["v_sb"][:, :]), 16)
    prs = [sb["pr0_sb"], sb["pr1_sb"]]
    pns = [sb["pn0_sb"], sb["pn1_sb"]]
    pts = [sb["pt0_sb"], sb["pt1_sb"]]
    avs = [sb["av0_sb"], sb["av1_sb"]]
    b1 = {}
    hs = {}
    for qt in range(NQ):
        ti = qt + 4
        slot = ti % 2
        if qt >= 2:
            p.wait("sync", "dv", b1[qt - 2]["xn"])
        else:
            p.wait("sync", "dv", a_end["dv"])
        vx = p.inc("ld", p.sync and p.sync.dma_start(
            out=xring[slot][:, :], in_=x3[ti]), 16)
        p.wait("vector", "ld", vx)
        if qt >= 1:
            p.wait("vector", "pe", b1[qt - 1]["trx"])
        vxn = p.inc("dv", p.vector and p.vector.scalar_tensor_tensor(
            xns[slot][:, :], xring[slot][:, :], sb["rstd_sb"][:, ti:ti + 1],
            sb["lnw_sb"][:, :], ALU.mult, ALU.mult))
        p.wait("tensor", "dv", vxn)
        if qt >= 1:
            p.wait("tensor", "dv", b1[qt - 1]["xnT_ev"])
        ins = None
        for fch in range(FCH):
            ins = p.tensor and p.tensor.transpose(
                psb[2][:, fch * 128:(fch + 1) * 128],
                xns[slot][:, fch * 128:(fch + 1) * 128], sb["ident_sb"][:, :])
        vtrx = p.inc("pe", ins)
        p.wait("vector", "pe", vtrx)
        if qt >= 1:
            p.wait("vector", "pe", b1[qt - 1]["qp"])
        vxnT = p.inc("dv", p.vector and p.vector.tensor_copy(
            sb["xnT_sb"][:, :], psb[2][:, 0:D]))
        p.wait("tensor", "dv", vxnT)
        p.wait("tensor", "cast", wq_ready)
        if qt >= 1:
            p.wait("tensor", "dv", b1[qt - 1]["rope_rd"])
        ins = None
        for fch in range(FCH):
            for ng in range(4):
                tgt = psf[2] if ng < 2 else psf[3]
                ins = p.tensor and p.tensor.matmul(
                    tgt[:, (ng % 2) * 512:(ng % 2) * 512 + 512],
                    sb["xnT_sb"][:, fch * 128:(fch + 1) * 128],
                    sb["wq_sb"][:, fch * D + ng * 512: fch * D + ng * 512 + 512],
                    start=(fch == 0), stop=(fch == FCH - 1))
        vqp = p.inc("pe", ins)
        p.wait("vector", "pe", vqp)
        if qt >= 1:
            p.wait("vector", "pe", b1[qt - 1]["q_done"])
        vrope_rd = None
        for hf in range(2):
            src = psf[2] if hf == 0 else psf[3]
            qv = src[:, 0:1024].rearrange("p (h c) -> p h c", h=8)
            c2s = sb["c2_sb"][:, ti * 128:(ti + 1) * 128].unsqueeze(1).broadcast_to([128, 8, 128])
            p.vector and p.vector.tensor_tensor(
                sb["t1_sb"][:, 0:1024].rearrange("p (h c) -> p h c", h=8), qv, c2s, ALU.mult)
            q4 = src[:, 0:1024].rearrange("p (h i pr) -> p h i pr", h=8, pr=2)
            t24 = sb["t2_sb"][:, 0:1024].rearrange("p (h i pr) -> p h i pr", h=8, pr=2)
            s24 = sb["s2_sb"][:, ti * 128:(ti + 1) * 128].rearrange(
                "p (i pr) -> p i pr", pr=2).unsqueeze(1).broadcast_to([128, 8, 64, 2])
            p.vector and p.vector.tensor_tensor(
                t24[:, :, :, 0:1], q4[:, :, :, 1:2], s24[:, :, :, 0:1], ALU.mult)
            vrope_rd = p.inc("dv", p.vector and p.vector.tensor_tensor(
                t24[:, :, :, 1:2], q4[:, :, :, 0:1], s24[:, :, :, 1:2], ALU.mult))
            p.wait("vector", "dv", vrope_rd)
            p.inc("dv", p.vector and p.vector.tensor_add(
                sb["q_sb"][:, hf * 1024:(hf + 1) * 1024], sb["t1_sb"][:, 0:1024],
                sb["t2_sb"][:, 0:1024]))
        vq_ready = p.cnt["dv"]
        p.wait("tensor", "dv", vq_ready)
        ins = None
        for h in range(H):
            ins = p.tensor and p.tensor.transpose(
                psb[2][:, h * 128:(h + 1) * 128],
                sb["q_sb"][:, h * 128:(h + 1) * 128], sb["ident_sb"][:, :])
        vtrq = p.inc("pe", ins)
        p.wait("vector", "pe", vtrq)
        if qt >= 1:
            p.wait("vector", "pe", b1[qt - 1]["q_done"])
        vqT = p.inc("dv", p.vector and p.vector.tensor_copy(
            sb["qT_sb"][:, :], psb[2][:, 0:D]))
        if DBG:
            p.wait("sync", "dv", vqT)
            p.inc("st", p.sync and p.sync.dma_start(
                out=dt["dbg_qt"][qt * 128:(qt + 1) * 128, :], in_=sb["qT_sb"][:, :]), 16)
        cls = qt if qt < 4 else 4
        for h in range(H):
            pp = h % 2
            pair_f = psf[pp]
            pair_b = psb[pp]
            kv = h // 4
            prev2 = hs.get((qt, h - 2)) or (hs.get((qt - 1, H - 2 + h)) if qt > 0 else None)
            prev1 = hs.get((qt, h - 1)) or (hs.get((qt - 1, H - 1)) if qt > 0 else None)
            p.wait("tensor", "dv", vqT)
            if prev2 is not None:
                p.wait("tensor", "ac", prev2["avev"])
                p.wait("tensor", "ac", prev2["ptev"])
                p.wait("tensor", "dv", prev2["nm"])
            kmov1 = sb["kT_sb"][:, :].rearrange("p (t c) -> p t c", c=512)[
                :, ti - 4:ti, kv * 128:(kv + 1) * 128]
            p.tensor and p.tensor.matmul(
                pair_f[:, 0:512], sb["qT_sb"][:, h * 128:(h + 1) * 128], kmov1,
                start=True, stop=True)
            vsc = p.inc("pe", p.tensor and p.tensor.matmul(
                pair_f[:, 512:640], sb["qT_sb"][:, h * 128:(h + 1) * 128],
                sb["kT_sb"][:, ti * 512 + kv * 128: ti * 512 + kv * 128 + 128],
                start=True, stop=True))
            p.wait("vector", "pe", vsc)
            so = pair_f[:, 0:640].rearrange("p (bk c) -> p bk c", bk=5)
            sm = sb["masks_sb"][:, cls * 640:(cls + 1) * 640].rearrange("p (bk c) -> p bk c", bk=5)
            if qt < 4:
                vmask = p.inc("dv", p.vector and p.vector.tensor_tensor(
                    pair_f[:, 0:640], pair_f[:, 0:640],
                    sb["masks_sb"][:, cls * 640:(cls + 1) * 640], ALU.add))
            else:
                vmask = p.inc("dv", p.vector and p.vector.tensor_tensor(
                    so[:, 0:5:4, :], so[:, 0:5:4, :], sm[:, 0:5:4, :], ALU.add))
            p.wait("vector", "dv", vmask)
            if DBG and qt == 4 and h == 0:
                vdbg = p.inc("dv", p.vector and p.vector.tensor_copy(sb["sq_sb"][:, 0:640], pair_f[:, 0:640]))
                p.wait("sync", "dv", vdbg)
                p.inc("st", p.sync and p.sync.dma_start(out=dt["dbg_sc"][:, :], in_=sb["sq_sb"][:, 0:640]), 16)
            if prev2 is not None:
                p.wait("vector", "ac", prev2["exp"])
            p.vector and p.vector.reduce_max(
                sb["nm2_sb"][:, 2 * pp:2 * pp + 1], pair_f[:, 0:512], axis=AX.X, negate=True)
            vr2 = p.inc("dv", p.vector and p.vector.reduce_max(
                sb["nm2_sb"][:, 2 * pp + 1:2 * pp + 2], pair_f[:, 512:640], axis=AX.X, negate=True))
            p.wait("vector", "dv", vr2)
            vnmm = p.inc("dv", p.vector and p.vector.tensor_reduce(
                sb["nm1_sb"][:, pp:pp + 1], sb["nm2_sb"][:, 2 * pp:2 * pp + 2], AX.X, ALU.min))
            p.wait("vector", "dv", vnmm)
            vnm = p.inc("dv", p.vector and p.vector.tensor_scalar(
                sb["nms_sb"][:, pp:pp + 1], sb["nm1_sb"][:, pp:pp + 1], SC, None, ALU.mult))
            p.wait("scalar", "dv", vnm)
            if prev2 is not None:
                p.wait("scalar", "dv", prev2["pn"])
            vexp = p.inc("ac", p.scalar and p.scalar.activation(
                prs[pp][:, :], pair_f[:, 0:640], AF.Exp, bias=sb["nms_sb"][:, pp:pp + 1],
                scale=SC, accum_out=sb["rows_sb"][:, pp:pp + 1]))
            p.wait("vector", "ac", vexp)
            vrc = p.inc("dv", p.vector and p.vector.reciprocal(
                sb["rec_sb"][:, pp:pp + 1], sb["rows_sb"][:, pp:pp + 1]))
            p.wait("vector", "dv", vrc)
            if prev2 is not None:
                p.wait("vector", "pe", prev2["tr"])
            vpn = p.inc("dv", p.vector and p.vector.tensor_scalar(
                pns[pp][:, :], prs[pp][:, :], sb["rec_sb"][:, pp:pp + 1], None, ALU.mult))
            if DBG and qt == 4 and h == 0:
                p.wait("sync", "dv", vpn)
                p.inc("st", p.sync and p.sync.dma_start(out=dt["dbg_pn"][:, :], in_=pns[pp][:, :]), 16)
            p.wait("tensor", "dv", vpn)
            p.wait("tensor", "ac", vexp)
            ins = None
            for j in range(5):
                ins = p.tensor and p.tensor.transpose(
                    pair_b[:, j * 128:(j + 1) * 128], pns[pp][:, j * 128:(j + 1) * 128],
                    sb["ident_sb"][:, :])
            vtr = p.inc("pe", ins)
            p.wait("scalar", "pe", vtr)
            if prev2 is not None:
                p.wait("scalar", "pe", prev2["av"])
            vptev = p.inc("ac", p.scalar and p.scalar.copy(
                pts[pp][:, :], pair_b[:, 0:640]))
            if DBG and qt == 4 and h == 0:
                p.wait("sync", "ac", vptev)
                p.inc("st", p.sync and p.sync.dma_start(out=dt["dbg_pt"][:, :], in_=pts[pp][:, :]), 16)
            p.wait("tensor", "ac", vptev)
            ins = None
            for j in range(5):
                tj = ti - 4 + j
                ins = p.tensor and p.tensor.matmul(
                    pair_f[:, 640:768],
                    sb["v_sb"][:, tj * 512 + kv * 128: tj * 512 + kv * 128 + 128],
                    pts[pp][:, j * 128:(j + 1) * 128], start=(j == 0), stop=(j == 4))
            vav = p.inc("pe", ins)
            p.wait("scalar", "pe", vav)
            if h == 0 and qt >= 2:
                p.wait("scalar", "st", b1[qt - 2]["av_st"])
            vavev = p.inc("ac", p.scalar and p.scalar.copy(
                avs[qt % 2][:, h * 128:(h + 1) * 128], pair_f[:, 640:768]))
            hs[(qt, h)] = dict(avev=vavev, ptev=vptev, exp=vexp, pn=vpn, tr=vtr,
                               av=vav, nm=vnm)
        q_done = p.cnt["pe"]
        p.wait("sync", "ac", hs[(qt, H - 1)]["avev"])
        vst = p.inc("st", p.sync and p.sync.dma_start(
            out=dt["av_d"][qt * 128:(qt + 1) * 128, :], in_=avs[qt % 2][:, :]), 16)
        if DBG:
            vst = p.inc("st", p.sync and p.sync.dma_start(
                out=dt["dbg_av"][qt * 128:(qt + 1) * 128, :], in_=avs[qt % 2][:, :]), 16)
        b1[qt] = dict(xn=vxn, trx=vtrx, xnT_ev=vxnT, rope_rd=vrope_rd,
                      q_done=q_done, av_st=vst, qp=vqp)
    b1_end = p.snap()

    p.wait("gpsimd", "pe", b1_end["pe"])
    wo3 = dt["wo"].rearrange("(c p) n -> p c n", p=128)
    p.inc("cast", p.gpsimd and p.gpsimd.dma_start(
        out=sb["wo_sb"][:, :].rearrange("p (c n) -> p c n", c=FCH), in_=wo3), 16)
    wo_ready = p.cnt["cast"]

    avbr = [sb["avb0_sb"], sb["avb1_sb"]]
    b2 = {}
    for qt in range(NQ):
        ti = qt + 4
        slot = qt % 2
        if qt == 0:
            p.wait("sync", "pe", b1_end["pe"])
        if qt < 8:
            for j in range(8):
                scale_unit(1, qt * 8 + j, bufsB, sb["sq_sb"])
        else:
            for u in range((qt - 8) * 8, (qt - 7) * 8):
                quant_unit(1, u, bufsB)
        if qt == 7:
            th_calc(1)
        p.wait("sync", "st", b1[qt]["av_st"])
        if qt >= 2:
            p.wait("sync", "pe", b2[qt - 2]["wo_done"])
        vav = p.inc("ld", p.sync and p.sync.dma_start(
            out=avbr[slot][:, :], in_=dt["av_d"][qt * 128:(qt + 1) * 128, :]), 16)
        if qt >= 2:
            p.wait("sync", "dv", b2[qt - 2]["z_free"])
        else:
            p.wait("sync", "dv", b1_end["dv"])
        vx = p.inc("ld", p.sync and p.sync.dma_start(
            out=xring[slot][:, :], in_=x3[ti]), 16)
        p.wait("tensor", "ld", vav)
        p.wait("tensor", "cast", wo_ready)
        if qt >= 1:
            p.wait("tensor", "dv", b2[qt - 1]["z_free"])
        ins = None
        for h in range(H):
            for ng in range(4):
                tgt = psf[0] if ng < 2 else psf[1]
                ins = p.tensor and p.tensor.matmul(
                    tgt[:, (ng % 2) * 512:(ng % 2) * 512 + 512],
                    avbr[slot][:, h * 128:(h + 1) * 128],
                    sb["wo_sb"][:, h * D + ng * 512: h * D + ng * 512 + 512],
                    start=(h == 0), stop=(h == H - 1))
        vwo = p.inc("pe", ins)
        p.wait("vector", "pe", vwo)
        p.wait("vector", "ld", vx)
        vz = None
        for ng in range(4):
            src = psf[0] if ng < 2 else psf[1]
            vz = p.inc("dv", p.vector and p.vector.scalar_tensor_tensor(
                sb["z_sb"][:, ng * 512:(ng + 1) * 512],
                src[:, (ng % 2) * 512:(ng % 2) * 512 + 512],
                1.0, xring[slot][:, ng * 512:(ng + 1) * 512], ALU.mult, ALU.add,
                accum_out=sb["zp_sb"][:, ng:ng + 1]))
        vz_last = vz
        p.wait("vector", "dv", vz)
        vzs = p.inc("dv", p.vector and p.vector.tensor_reduce(
            sb["st1_sb"][:, 0:1], sb["zp_sb"][:, 0:4], AX.X, ALU.add))
        p.wait("vector", "dv", vzs)
        vzm = p.inc("dv", p.vector and p.vector.tensor_scalar(
            sb["st1_sb"][:, 1:2], sb["st1_sb"][:, 0:1], 1.0 / D, None, ALU.mult))
        p.wait("scalar", "dv", vz)
        vsq = p.inc("ac", p.scalar and p.scalar.activation(
            sb["sq_sb"][:, :], sb["z_sb"][:, :], AF.Square, accum_out=sb["st1_sb"][:, 2:3]))
        p.wait("vector", "ac", vsq)
        p.wait("vector", "dv", vzm)
        vm2 = p.inc("dv", p.vector and p.vector.tensor_mul(
            sb["st1_sb"][:, 3:4], sb["st1_sb"][:, 1:2], sb["st1_sb"][:, 1:2]))
        p.wait("vector", "dv", vm2)
        vv1 = p.inc("dv", p.vector and p.vector.tensor_scalar(
            sb["st1_sb"][:, 2:3], sb["st1_sb"][:, 2:3], 1.0 / D, None, ALU.mult))
        p.wait("vector", "dv", vv1)
        vvar = p.inc("dv", p.vector and p.vector.tensor_sub(
            sb["st1_sb"][:, 2:3], sb["st1_sb"][:, 2:3], sb["st1_sb"][:, 3:4]))
        p.wait("scalar", "dv", vvar)
        p.scalar and p.scalar.activation(
            sb["st1_sb"][:, 2:3], sb["st1_sb"][:, 2:3], AF.Ln,
            bias=sb["cst_sb"][:, 1:2], scale=1.0)
        vrstd = p.inc("ac", p.scalar and p.scalar.activation(
            sb["st1_sb"][:, 3:4], sb["st1_sb"][:, 2:3], AF.Exp, scale=-0.5))
        p.wait("vector", "ac", vrstd)
        if qt >= 1:
            p.wait("vector", "st", b2[qt - 1]["y_st"])
        vy = p.inc("dv", p.vector and p.vector.tensor_scalar(
            sb["y_sb"][:, :], sb["z_sb"][:, :], sb["st1_sb"][:, 1:2], sb["st1_sb"][:, 3:4],
            ALU.subtract, ALU.mult))
        p.wait("vector", "dv", vy)
        vyb = p.inc("dv", p.vector and p.vector.tensor_copy(
            sb["ybf_sb"][:, :], sb["y_sb"][:, :]))
        p.wait("tensor", "dv", vyb)
        if qt >= 1:
            p.wait("tensor", "dv", b2[qt - 1]["ytt_ev"])
        ins = None
        for fch in range(FCH):
            ins = p.tensor and p.tensor.transpose(
                psb[2][:, fch * 128:(fch + 1) * 128],
                sb["ybf_sb"][:, fch * 128:(fch + 1) * 128], sb["ident_sb"][:, :])
        vtry = p.inc("pe", ins)
        p.wait("vector", "pe", vtry)
        if qt >= 1:
            p.wait("vector", "st", b2[qt - 1]["ytt_st"])
        vytt = p.inc("dv", p.vector and p.vector.tensor_copy(
            sb["ytt_sb"][:, :], psb[2][:, 0:D]))
        p.wait("sync", "dv", vy)
        vys = p.inc("st", p.sync and p.sync.dma_start(
            out=dt["y_d"][qt * 128:(qt + 1) * 128, :], in_=sb["y_sb"][:, :]), 16)
        if DBG:
            vys = p.inc("st", p.sync and p.sync.dma_start(
                out=dt["dbg_y"][qt * 128:(qt + 1) * 128, :], in_=sb["y_sb"][:, :]), 16)
        p.wait("sync", "dv", vytt)
        yt3 = dt["yt_d"].rearrange("(c p) t -> p c t", p=128)[:, :, qt * 128:(qt + 1) * 128]
        vyts = p.inc("st", p.sync and p.sync.dma_start(
            out=yt3, in_=sb["ytt_sb"][:, :].rearrange("p (c t) -> p c t", c=FCH)), 16)
        b2[qt] = dict(wo_done=vwo, z_free=vz_last, ytt_ev=vytt, ytt_st=vyts, y_st=vys)
    b2_end = p.snap()

    for e in P.ENGS:
        p.wait(e, "pe", b2_end["pe"])
        p.wait(e, "dv", b2_end["dv"])
        p.wait(e, "ac", b2_end["ac"])
        p.wait(e, "st", b2_end["st"])

    w1r = [sb[f"w1r{i}_sb"] for i in range(3)]
    w2r = [sb[f"w2r{i}_sb"] for i in range(3)]
    ofr = [sb["of0_sb"], sb["of1_sb"]]
    ffn = {}
    for q4 in range(4):
        if q4 >= 1:
            p.wait("sync", "pe", ffn[(q4 - 1, "m1_last")])
            p.wait("sync", "dv", ffn[(q4 - 1, "ln_done")])
        yt_src = dt["yt_d"].rearrange("(c p) t -> p c t", p=128)[:, :, q4 * 512:(q4 + 1) * 512]
        vyt = p.inc("ld", p.sync and p.sync.dma_start(
            out=sb["yT_sb"][:, :].rearrange("p (c t) -> p c t", c=FCH), in_=yt_src), 16)
        y_src = dt["y_d"].rearrange("(t p) d -> p t d", p=128)[:, q4 * 4:(q4 + 1) * 4, :]
        vo2 = p.inc("ld", p.sync and p.sync.dma_start(
            out=sb["out2_sb"][:, :].rearrange("p (t d) -> p t d", t=4), in_=y_src), 16)
        p.wait("vector", "ld", vo2)
        o2v = sb["out2_sb"][:, :].rearrange("p (t d) -> p t d", t=4)
        b2v = sb["b2r_sb"][:, :].unsqueeze(1).broadcast_to([128, 4, D])
        vb2 = p.inc("dv", p.vector and p.vector.tensor_tensor(o2v, o2v, b2v, ALU.add))
        vm1 = None
        for fc in range(NFC):
            slot = fc % 3
            if fc >= 3:
                prevk = (q4, fc - 3)
            elif q4 > 0:
                prevk = (q4 - 1, [63, 61, 62][fc])
            else:
                prevk = None
            if prevk is not None and (prevk[0], "m1", prevk[1]) in ffn:
                p.wait("sync", "pe", ffn[(prevk[0], "m1", prevk[1])])
            vw1 = p.inc("ld", p.sync and p.sync.dma_start(
                out=w1r[slot][:, :], in_=dt["w1q_d"][fc * 128:(fc + 1) * 128, :]), 16)
            p.wait("tensor", "ld", vw1)
            p.wait("tensor", "ld", vyt)
            if q4 == 0 and fc == 0:
                p.wait("tensor", "st", w1q_st_done)
            hps = psf[fc % 2][:, 0:512]
            gp = ffn.get((q4, "gelu", fc - 2)) or (ffn.get((q4 - 1, "gelu", NFC - 2 + fc)) if q4 > 0 else None)
            if gp:
                p.wait("tensor", "ac", gp)
            if q4 > 0 and fc < 2:
                p.wait("tensor", "dv", ffn[(q4 - 1, "flush_last")])
            ins = None
            for fch in range(FCH):
                ins = p.tensor and p.tensor.matmul(
                    hps, w1r[slot][:, fch * 128:(fch + 1) * 128],
                    sb["yT_sb"][:, fch * 512:(fch + 1) * 512],
                    start=(fch == 0), stop=(fch == FCH - 1))
            vm1 = p.inc("pe", ins)
            ffn[(q4, "m1", fc)] = vm1
            p.wait("scalar", "pe", vm1)
            pm2 = ffn.get((q4 - 1, "m2_last")) if q4 > 0 and fc < 2 else None
            if pm2:
                p.wait("scalar", "pe", pm2)
            vgel = p.inc("ac", p.scalar and p.scalar.activation(
                sb["h_sb"][:, fc * 512:(fc + 1) * 512], hps, AF.Gelu,
                bias=sb["b1_sb"][:, fc:fc + 1], scale=0.5))
            ffn[(q4, "gelu", fc)] = vgel
        ffn[(q4, "m1_last")] = vm1
        gelu_all = p.cnt["ac"]
        vfl = None
        vm2 = None
        w2q4 = dt["w2q_d"].rearrange("(g p) d -> p g d", p=128)
        for ng in range(4):
            pool = [psf[2], psf[3]] if ng % 2 == 0 else [psf[0], psf[1]]
            for fc in range(NFC):
                grp = fc // 4
                slot3 = grp % 3
                if fc % 4 == 0:
                    if grp >= 3:
                        prevu = ffn.get((q4, ng, grp - 3))
                    else:
                        pg = [15, 13, 14][grp]
                        prevu = (ffn.get((q4, ng - 1, pg)) if ng > 0 else
                                 (ffn.get((q4 - 1, 3, pg)) if q4 > 0 else None))
                    if prevu is not None:
                        p.wait("sync", "pe", prevu)
                    vw2 = p.inc("ld", p.sync and p.sync.dma_start(
                        out=w2r[slot3][:, :].rearrange("p (g c) -> p g c", g=4),
                        in_=w2q4[:, fc:fc + 4, ng * 512:(ng + 1) * 512]), 16)
                p.wait("tensor", "ld", vw2)
                p.wait("tensor", "ac", ffn[(q4, "gelu", fc)])
                if ng % 2 == 1 and fc == 0:
                    p.wait("tensor", "ac", gelu_all)
                if fc == 0:
                    fp = ffn.get((q4, "flush", ng - 2))
                    if fp:
                        p.wait("tensor", "dv", fp)
                ins = None
                for tt in range(4):
                    tgt = pool[tt // 2]
                    ins = p.tensor and p.tensor.matmul(
                        tgt[:, (tt % 2) * 512:(tt % 2) * 512 + 512],
                        sb["h_sb"][:, fc * 512 + tt * 128: fc * 512 + tt * 128 + 128],
                        w2r[slot3][:, (fc % 4) * 512:(fc % 4) * 512 + 512],
                        start=(fc == 0), stop=(fc == NFC - 1))
                vm2 = p.inc("pe", ins)
                if fc % 4 == 3 or fc == NFC - 1:
                    ffn[(q4, ng, grp)] = vm2
            ffn[(q4, "m2", ng)] = vm2
            p.wait("vector", "pe", vm2)
            p.wait("vector", "dv", vb2)
            for tt in range(4):
                src = pool[tt // 2]
                ins = p.vector and p.vector.scalar_tensor_tensor(
                    sb["out2_sb"][:, tt * D + ng * 512: tt * D + ng * 512 + 512],
                    src[:, (tt % 2) * 512:(tt % 2) * 512 + 512], 0.5,
                    sb["out2_sb"][:, tt * D + ng * 512: tt * D + ng * 512 + 512],
                    ALU.mult, ALU.add)
                vfl = p.inc("dv", ins)
            ffn[(q4, "flush", ng)] = vfl
        ffn[(q4, "m2_last")] = vm2
        ffn[(q4, "flush_last")] = vfl
        if DBG:
            p.wait("sync", "dv", vfl)
            p.inc("st", p.sync and p.sync.dma_start(
                out=dt["dbg_o2"].rearrange("(t p) d -> p t d", p=128)[:, q4 * 4:(q4 + 1) * 4, :],
                in_=sb["out2_sb"][:, :].rearrange("p (t d) -> p t d", t=4)), 16)
        vln = None
        for tt in range(4):
            p.wait("scalar", "dv", ffn[(q4, "flush", 3)])
            vsum = p.inc("ac", p.scalar and p.scalar.activation(
                sb["fsq_sb"][:, :], sb["out2_sb"][:, tt * D:(tt + 1) * D], AF.Copy,
                accum_out=sb["st1_sb"][:, 0:1]))
            vsq = p.inc("ac", p.scalar and p.scalar.activation(
                sb["fsq_sb"][:, :], sb["out2_sb"][:, tt * D:(tt + 1) * D], AF.Square,
                accum_out=sb["st1_sb"][:, 2:3]))
            p.wait("vector", "ac", vsq)
            vzm = p.inc("dv", p.vector and p.vector.tensor_scalar(
                sb["st1_sb"][:, 1:2], sb["st1_sb"][:, 0:1], 1.0 / D, None, ALU.mult))
            p.wait("vector", "dv", vzm)
            vm2_ = p.inc("dv", p.vector and p.vector.tensor_mul(
                sb["st1_sb"][:, 3:4], sb["st1_sb"][:, 1:2], sb["st1_sb"][:, 1:2]))
            p.wait("vector", "dv", vm2_)
            vv1 = p.inc("dv", p.vector and p.vector.tensor_scalar(
                sb["st1_sb"][:, 2:3], sb["st1_sb"][:, 2:3], 1.0 / D, None, ALU.mult))
            p.wait("vector", "dv", vv1)
            vvar = p.inc("dv", p.vector and p.vector.tensor_sub(
                sb["st1_sb"][:, 2:3], sb["st1_sb"][:, 2:3], sb["st1_sb"][:, 3:4]))
            p.wait("scalar", "dv", vvar)
            p.scalar and p.scalar.activation(
                sb["st1_sb"][:, 2:3], sb["st1_sb"][:, 2:3], AF.Ln,
                bias=sb["cst_sb"][:, 1:2], scale=1.0)
            vrstd = p.inc("ac", p.scalar and p.scalar.activation(
                sb["st1_sb"][:, 3:4], sb["st1_sb"][:, 2:3], AF.Exp, scale=-0.5))
            p.wait("vector", "ac", vrstd)
            pst = ffn.get((q4, "ofst", tt - 2))
            if pst:
                p.wait("vector", "st", pst)
            vout = p.inc("dv", p.vector and p.vector.tensor_scalar(
                ofr[tt % 2][:, :], sb["out2_sb"][:, tt * D:(tt + 1) * D],
                sb["st1_sb"][:, 1:2], sb["st1_sb"][:, 3:4], ALU.subtract, ALU.mult))
            vln = vout
            p.wait("sync", "dv", vout)
            row0 = (q4 * 4 + tt) * 128
            vst = p.inc("st", p.sync and p.sync.dma_start(
                out=dt["out_sh"][row0:row0 + 128, :], in_=ofr[tt % 2][:, :]), 16)
            ffn[(q4, "ofst", tt)] = vst
        ffn[(q4, "ln_done")] = vln

    for e in P.ENGS:
        p.wait(e, "st")


_NC_CACHE = {}


def kernel(**inputs):
    in_maps = _stage_inputs(**inputs)
    if "main" not in _NC_CACHE:
        _NC_CACHE["main"] = build_nc()
    nc = _NC_CACHE["main"]
    res = run_bass_kernel_spmd(nc, in_maps, core_ids=list(range(8)))
    outs = [r["out_sh"] for r in res.results]
    full = np.concatenate(outs, axis=0).reshape(4, 4096, D).astype(np.float32)
    if DBG:
        kernel._dbg = res.results
    return full
